# version 15
# baseline (speedup 1.0000x reference)
import numpy as np
import ml_dtypes
from contextlib import ExitStack

import concourse.bass as bass
import concourse.mybir as mybir
from concourse.bass_utils import run_bass_kernel_spmd

F32 = mybir.dt.float32
BF16 = mybir.dt.bfloat16
AF = mybir.ActivationFunctionType
ALU = mybir.AluOpType
AX = mybir.AxisListType

D = 1024
HEADS = 16
DK = 32
DV = 64
QF = HEADS * DK
C = 128
ST1 = 512
ST2 = 1024
E = 16
M = 512
EPS = 1e-5
GATE_NORM = 16.0
DKS = DK ** -0.5

DEBUG_TAPS = []
P1L = 99
P1G = 99
EXP_NOCACHE = False


class Ctr:
    def __init__(self, es, nc, name, step=1):
        self.h = es.enter_context(nc.semaphore(name))
        self.step = step
        self.n = 0

    def bump(self, k=1):
        self.n += k * self.step
        return (self, self.n)

    @property
    def tok(self):
        return (self, self.n)


class EQ:
    def __init__(self, name):
        self.name = name
        self.ops = []
        self.waited = {}

    def wait(self, tok):
        if tok is None:
            return
        if isinstance(tok, (list, tuple)) and tok and isinstance(tok[0], (list, tuple)):
            for t in tok:
                self.wait(t)
            return
        ctr, v = tok
        if v <= 0 or self.waited.get(id(ctr), -1) >= v:
            return
        self.waited[id(ctr)] = v
        h = ctr.h
        self.ops.append(lambda e, h=h, v=v: e.wait_ge(h, v))

    def do(self, fn, sig=None, deps=(), k=1):
        for d in deps:
            self.wait(d)
        if sig is not None and sig.step == 16:
            self.wait(sig.tok)
        if sig is not None:
            tok = sig.bump(k)
            h, step = sig.h, sig.step * k
            self.ops.append(lambda e, fn=fn, h=h, s=step: fn(e).then_inc(h, s))
            return tok
        self.ops.append(lambda e, fn=fn: fn(e))
        return None


def _bc(ap, shape):
    return ap.to_broadcast(shape)


def build_program(T_core, taps=(), phases=3):
    taps = set(taps)
    nc = bass.Bass("TRN2", target_bir_lowering=False)
    NST1 = T_core // ST1
    ST2_ = ST2 if T_core >= ST2 else T_core
    NST2 = (T_core // ST2_) if (phases & 2) else 0
    NC1 = ST1 // C
    NC2 = ST2_ // C

    xin_h = nc.dram_tensor("x", [T_core, D], F32, kind="ExternalInput").ap()
    out_h = nc.dram_tensor("out", [T_core, D], F32, kind="ExternalOutput").ap()
    wq_h = nc.dram_tensor("Wq", [D, QF], F32, kind="ExternalInput").ap()
    wk_h = nc.dram_tensor("Wk", [D, QF], F32, kind="ExternalInput").ap()
    wv_h = nc.dram_tensor("Wv", [D, D], F32, kind="ExternalInput").ap()
    wg_h = nc.dram_tensor("Wg", [D, D], F32, kind="ExternalInput").ap()
    wo_h = nc.dram_tensor("Wo", [D, D], F32, kind="ExternalInput").ap()
    w1_h = nc.dram_tensor("W1", [D, 2 * D], F32, kind="ExternalInput").ap()
    w2_h = nc.dram_tensor("W2", [2 * D, D], F32, kind="ExternalInput").ap()
    gk1_h = nc.dram_tensor("gk1pad", [D, 32], F32, kind="ExternalInput").ap()
    gk2_h = nc.dram_tensor("gk2ext", [32, QF], F32, kind="ExternalInput").ap()
    lnw_h = nc.dram_tensor("ln_w", [1, D], F32, kind="ExternalInput").ap()
    lnb_h = nc.dram_tensor("ln_b", [1, D], F32, kind="ExternalInput").ap()
    rmsw_h = nc.dram_tensor("rms_w", [1, DV], F32, kind="ExternalInput").ap()
    b1_h = nc.dram_tensor("b1", [128, 16], F32, kind="ExternalInput").ap()
    b2_h = nc.dram_tensor("b2", [1, D], F32, kind="ExternalInput").ap()
    gwt_h = nc.dram_tensor("gate_wT", [D, E], F32, kind="ExternalInput").ap()
    ewg_h = nc.dram_tensor("e_wg", [E, D, M], F32, kind="ExternalInput").ap()
    ewu_h = nc.dram_tensor("e_wu", [E, D, M], F32, kind="ExternalInput").ap()
    ewd_h = nc.dram_tensor("e_wd", [E, M, D], F32, kind="ExternalInput").ap()
    swg_h = nc.dram_tensor("ws_gate", [D, M], F32, kind="ExternalInput").ap()
    swu_h = nc.dram_tensor("ws_up", [D, M], F32, kind="ExternalInput").ap()
    swd_h = nc.dram_tensor("ws_down", [M, D], F32, kind="ExternalInput").ap()
    x1scr = nc.dram_tensor("x1scr", [T_core, D], F32, kind="Internal").ap()
    w1c_h = nc.dram_tensor("w1cache", [4, 128, 8 * 512], BF16,
                           kind="Internal").ap()
    w2c_h = nc.dram_tensor("w2cache", [4, 128, 4 * 1024], BF16,
                           kind="Internal").ap()
    ewgc_h = nc.dram_tensor("ewgc", [E, 128, 8 * M], BF16, kind="Internal").ap()
    ewuc_h = nc.dram_tensor("ewuc", [E, 128, 8 * M], BF16, kind="Internal").ap()
    ewdc_h = nc.dram_tensor("ewdc", [E, 128, 4 * D], BF16, kind="Internal").ap()
    tap_h = {}
    for t in taps:
        tap_h[t] = nc.dram_tensor("tap_" + t, [T_core, 2 * D], F32,
                                  kind="ExternalOutput").ap()

    iden_b = nc.inline_tensor(np.eye(C, dtype=ml_dtypes.bfloat16), name="idenb").ap()
    iden_f = nc.inline_tensor(np.eye(C, dtype=np.float32), name="idenf").ap()
    um = np.zeros((C, C), np.float32)
    for s_ in range(C):
        um[s_, s_:] = -1.0 / GATE_NORM
    um_c = nc.inline_tensor(um, name="umtri").ap()
    mAT = np.zeros((C, C), np.float32)
    for s_ in range(C):
        mAT[s_, s_:] = 1.0
    mAT_c = nc.inline_tensor(mAT, name="maskAT").ap()
    onescol_c = nc.inline_tensor(np.ones((1, C), np.float32), name="onescol").ap()
    onescolb_c = nc.inline_tensor(
        np.ones((1, C), ml_dtypes.bfloat16), name="onescolb").ap()
    negsix_c = nc.inline_tensor(
        np.full((C, 1), -1.0 / GATE_NORM, np.float32), name="negsix").ap()
    e16s = np.zeros((1, E * E), np.float32)
    for h in range(E):
        e16s[0, E * h + h] = 1.0
    e16s_c = nc.inline_tensor(e16s, name="e16s").ap()
    e16col = np.zeros((1, 32), ml_dtypes.bfloat16)
    e16col[0, 16] = 1.0
    e16col_c = nc.inline_tensor(e16col, name="e16col").ap()
    onesrow_b = nc.inline_tensor(
        np.ones((1, ST1), ml_dtypes.bfloat16), name="onesrowb").ap()

    es = ExitStack()
    with es:
        sT = Ctr(es, nc, "sT")
        sV = Ctr(es, nc, "sV")
        sS = Ctr(es, nc, "sS")
        sP = Ctr(es, nc, "sP")
        dY = Ctr(es, nc, "dY", 16)
        dS = Ctr(es, nc, "dS", 16)
        dP = Ctr(es, nc, "dP", 16)
        dR = Ctr(es, nc, "dR", 16)
        dX = Ctr(es, nc, "dX", 16)
        dA = Ctr(es, nc, "dA", 16)
        dCW = Ctr(es, nc, "dCW", 16)
        dEC = Ctr(es, nc, "dEC", 16)
        dT = Ctr(es, nc, "dT", 16)
        dX2 = Ctr(es, nc, "dX2", 16)
        dA2 = Ctr(es, nc, "dA2", 16)
        w1s = [Ctr(es, nc, f"w1s{i}", 16) for i in range(2)]
        w2s = [Ctr(es, nc, f"w2s{i}", 16) for i in range(2)]
        eSs = [[Ctr(es, nc, f"eS{i}_{j}", 16) for j in range(3)]
               for i in range(3)]
        w1y = [Ctr(es, nc, f"w1y{i}", 16) for i in range(2)]
        w2y = [Ctr(es, nc, f"w2y{i}", 16) for i in range(2)]
        eYs = [[Ctr(es, nc, f"eY{i}_{j}", 16) for j in range(3)]
               for i in range(3)]
        qT, qV, qS, qP, qY = EQ("T"), EQ("V"), EQ("S"), EQ("P"), EQ("Y")
        pf_tok = {}
        pre_ln1 = {}

        pAT = es.enter_context(nc.psum_tensor("pAT", [128, 1024], F32))
        pO = es.enter_context(nc.psum_tensor("pO", [128, 1024], F32))
        pP0 = es.enter_context(nc.psum_tensor("pP0", [128, 512], F32))
        pP1 = es.enter_context(nc.psum_tensor("pP1", [128, 512], F32))
        pT = es.enter_context(nc.psum_tensor("pT", [128, 1024], BF16))
        pQx = es.enter_context(nc.psum_tensor("pQx", [128, 512], F32))
        pP = [pP0, pP1]
        guard = {}

        def bw(q, name):
            q.wait(guard.get(name))

        wtile = es.enter_context(nc.sbuf_tensor("wtile", [128, D], BF16))
        btile = es.enter_context(nc.sbuf_tensor("btile", [128, D], BF16))
        rmstile = es.enter_context(nc.sbuf_tensor("rmstile", [128, D], BF16))
        b1col = es.enter_context(nc.sbuf_tensor("b1col", [128, 16], F32))
        b2row = es.enter_context(nc.sbuf_tensor("b2row", [1, D], BF16))
        onesb_s = es.enter_context(nc.sbuf_tensor("onesb_s", [1, C], BF16))
        cst1 = es.enter_context(nc.sbuf_tensor("cst1", [128, 1], F32))
        iden_s = es.enter_context(nc.sbuf_tensor("iden_s", [128, C], BF16))
        idenf_s = es.enter_context(nc.sbuf_tensor("idenf_s", [128, C], F32))
        um_s = es.enter_context(nc.sbuf_tensor("um_s", [128, C], F32))
        mAT_s = es.enter_context(nc.sbuf_tensor("mAT_s", [128, C], F32))
        ones_s = es.enter_context(nc.sbuf_tensor("ones_s", [1, C], F32))
        nsix_s = es.enter_context(nc.sbuf_tensor("nsix_s", [128, 1], F32))
        e16s_s = es.enter_context(nc.sbuf_tensor("e16s_s", [1, E * E], F32))
        e16c_s = es.enter_context(nc.sbuf_tensor("e16c_s", [1, 32], BF16))
        onesr_s = es.enter_context(nc.sbuf_tensor("onesr_s", [1, ST1], BF16))
        es0 = ExitStack()
        lnw_r = es0.enter_context(nc.sbuf_tensor("lnw_r", [1, D], F32))
        lnb_r = es0.enter_context(nc.sbuf_tensor("lnb_r", [1, D], F32))
        rms_r = es0.enter_context(nc.sbuf_tensor("rms_r", [1, DV], F32))

        for dst, src in [
            (iden_s[:], iden_b[:]), (idenf_s[:], iden_f[:]),
            (um_s[:], um_c[:]), (mAT_s[:], mAT_c[:]),
            (ones_s[:], onescol_c[:]),
            (nsix_s[:], negsix_c[:]), (e16s_s[:], e16s_c[:]),
            (onesb_s[:], onescolb_c[:]),
            (e16c_s[:], e16col_c[:]), (onesr_s[:], onesrow_b[:]),
            (lnw_r[:], lnw_h[:]), (lnb_r[:], lnb_h[:]),
            (rms_r[:], rmsw_h[:]),
            (b1col[:], b1_h[:]),
        ]:
            qY.do(lambda y, d=dst, s=src: y.dma_start(out=d, in_=s), sig=dY)
        c_loaded = dY.tok

        qV.do(lambda v: v.memset(cst1[:], 1.0), sig=sV)
        qP.do(lambda g: g.dma_start(out=b2row[:], in_=b2_h[:]), sig=dR)

        def bcast_row(row, width, dst):
            w = min(512, width)
            nh = max(1, width // 512)
            for half in range(nh):
                bw(qT, "pQx")
                t0 = qT.do(lambda t, h=half, w=w, r=row: t.matmul(
                    pQx[:, 0:w], ones_s[0:1, :], r[0:1, h * 512:h * 512 + w],
                    start=True, stop=True), sig=sT, deps=[c_loaded])
                tok = qV.do(lambda v, h=half, w=w: v.tensor_copy(
                    dst[:, h * 512:h * 512 + w], pQx[:, 0:w]), sig=sV,
                    deps=[t0])
                guard["pQx"] = tok

        bcast_row(lnw_r, D, wtile)
        bcast_row(lnb_r, D, btile)
        bcast_row(rms_r, DV, rmstile)
        qV.do(lambda v: v.tensor_copy(
            rmstile[:].rearrange("p (h v) -> p h v", h=HEADS)[:, 1:, :],
            _bc(rmstile[:, None, 0:DV], [128, HEADS - 1, DV])), sig=sV,
            deps=[sV.tok])
        consts_ready = [sV.tok, sT.tok, c_loaded]
        es0.close()

        blk = es.enter_context(nc.Block())

        def flush():
            if qT.ops:
                ops = qT.ops; qT.ops = []
                blk.tensor(lambda t, ops=ops: [op(t) for op in ops])
            if qV.ops:
                ops = qV.ops; qV.ops = []
                blk.vector(lambda v, ops=ops: [op(v) for op in ops])
            if qS.ops:
                ops = qS.ops; qS.ops = []
                blk.scalar(lambda s, ops=ops: [op(s) for op in ops])
            if qP.ops:
                ops = qP.ops; qP.ops = []
                blk.gpsimd(lambda g, ops=ops: [op(g) for op in ops])
            if qY.ops:
                ops = qY.ops; qY.ops = []
                blk.sync(lambda y, ops=ops: [op(y) for op in ops])

        es1 = ExitStack()
        with es1:
            wq_s = es1.enter_context(nc.sbuf_tensor("wq_s", [128, 8 * 512], BF16))
            wk_s = es1.enter_context(nc.sbuf_tensor("wk_s", [128, 8 * 512], BF16))
            wv_s = es1.enter_context(nc.sbuf_tensor("wv_s", [128, 8 * 1024], BF16))
            wgg_s = es1.enter_context(nc.sbuf_tensor("wgg_s", [128, 8 * 1024], BF16))
            wo_s = es1.enter_context(nc.sbuf_tensor("wo_s", [128, 8 * 1024], BF16))
            gk1_s = es1.enter_context(nc.sbuf_tensor("gk1_s", [128, 8 * 32], BF16))
            gk2_s = es1.enter_context(nc.sbuf_tensor("gk2_s", [32, QF], BF16))
            w1b = [es1.enter_context(
                nc.sbuf_tensor(f"w1b{i}", [128, 8 * 512], BF16)) for i in range(2)]
            w2b = [es1.enter_context(
                nc.sbuf_tensor(f"w2b{i}", [128, 4 * 1024], BF16)) for i in range(2)]

            def cast_load(dst_view, src_view, ctr, deps=()):
                return qP.do(lambda g, d=dst_view, s=src_view: g.dma_start(
                    out=d, in_=s), sig=ctr, deps=deps)

            cast_load(wq_s[:].rearrange("p (k n) -> p k n", k=8),
                      wq_h.rearrange("(k p) n -> p k n", p=128), dR,
                      deps=consts_ready)
            cast_load(wk_s[:].rearrange("p (k n) -> p k n", k=8),
                      wk_h.rearrange("(k p) n -> p k n", p=128), dR)
            cast_load(wv_s[:].rearrange("p (k n) -> p k n", k=8),
                      wv_h.rearrange("(k p) n -> p k n", p=128), dR)
            cast_load(wgg_s[:].rearrange("p (k n) -> p k n", k=8),
                      wg_h.rearrange("(k p) n -> p k n", p=128), dR)
            cast_load(wo_s[:].rearrange("p (k n) -> p k n", k=8),
                      wo_h.rearrange("(k p) n -> p k n", p=128), dR)
            cast_load(gk1_s[:].rearrange("p (k q) -> p k q", k=8),
                      gk1_h.rearrange("(k p) q -> p k q", p=128), dR)
            cast_load(gk2_s[:], gk2_h[:], dR)
            attnw_ready = dR.tok

            xc_buf = [es1.enter_context(
                nc.sbuf_tensor(f"xc_buf{i}", [128, D], F32)) for i in range(2)]
            xs = es1.enter_context(nc.sbuf_tensor("xs", [128, NC1 * D], F32))
            hT = es1.enter_context(nc.sbuf_tensor("hT", [128, 8 * ST1], BF16))
            htok = es1.enter_context(nc.sbuf_tensor("htok", [128, D], BF16))
            qTf = es1.enter_context(nc.sbuf_tensor("qTf", [128, NC1 * 512], BF16))
            kTf = es1.enter_context(nc.sbuf_tensor("kTf", [128, NC1 * 512], BF16))
            qgT_hf = es1.enter_context(nc.sbuf_tensor("qgT_hf", [32, 8 * C], BF16))
            kgT_hf = es1.enter_context(nc.sbuf_tensor("kgT_hf", [32, 8 * C], BF16))
            vTok = [es1.enter_context(
                nc.sbuf_tensor(f"vTok{i}", [128, D], BF16)) for i in range(2)]
            gs = [es1.enter_context(
                nc.sbuf_tensor(f"gs{i}", [128, D], BF16)) for i in range(2)]
            gktok = es1.enter_context(nc.sbuf_tensor("gktok", [128, NC1 * QF], F32))
            eq_c = es1.enter_context(nc.sbuf_tensor("eq_c", [128, 4 * C], F32))
            ek_c = es1.enter_context(nc.sbuf_tensor("ek_c", [128, 4 * C], F32))
            kdt_c = [es1.enter_context(
                nc.sbuf_tensor(f"kdt_c{i}", [128, 512], BF16))
                for i in range(2)]
            atm_c = es1.enter_context(nc.sbuf_tensor("atm_c", [128, HEADS * C], BF16))
            ogT_c = es1.enter_context(nc.sbuf_tensor("ogT_c", [128, 8 * C], BF16))
            gelu_q = [es1.enter_context(
                nc.sbuf_tensor(f"gelu_q{i}", [128, 4 * ST1], BF16))
                for i in range(2)]
            t1ext = es1.enter_context(nc.sbuf_tensor("t1ext", [32, ST1], BF16))
            S_sb = es1.enter_context(nc.sbuf_tensor("S_sb", [32, HEADS * DV], F32))
            S_bf = es1.enter_context(nc.sbuf_tensor("S_bf", [32, HEADS * DV], BF16))
            dec_sb = [es1.enter_context(
                nc.sbuf_tensor(f"dec_sb{i}", [32, HEADS], F32))
                for i in range(2)]
            st_neg = es1.enter_context(nc.sbuf_tensor("st_neg", [128, 8], F32))
            st_ssq = es1.enter_context(nc.sbuf_tensor("st_ssq", [128, 8], F32))
            st_a = es1.enter_context(nc.sbuf_tensor("st_a", [128, 8], F32))
            st_b = es1.enter_context(nc.sbuf_tensor("st_b", [128, 8], F32))
            st_c = es1.enter_context(nc.sbuf_tensor("st_c", [128, 8], F32))
            st_d = es1.enter_context(nc.sbuf_tensor("st_d", [128, 8], F32))
            ss16 = es1.enter_context(nc.sbuf_tensor("ss16", [128, HEADS], F32))
            ri16 = es1.enter_context(nc.sbuf_tensor("ri16", [128, HEADS], F32))
            rg16 = es1.enter_context(nc.sbuf_tensor("rg16", [128, HEADS], F32))
            tscr = es1.enter_context(nc.sbuf_tensor("tscr", [128, 512], F32))
            tscr2 = es1.enter_context(nc.sbuf_tensor("tscr2", [128, 512], F32))
            esb = tscr

            qV.do(lambda v: v.memset(S_sb[:], 0.0), sig=sV)
            qV.do(lambda v: v.memset(S_bf[:], 0.0), sig=sV)

            x1_stored = []

            def tap(name, buf_view, rows, idx, toks, width):
                if name not in taps:
                    return
                qP.do(lambda g, b=buf_view, w=width, i=idx, r=rows:
                      g.dma_start(out=tap_h[name][i * r:(i + 1) * r, 0:w],
                                  in_=b), sig=dT, deps=toks)

            I32 = mybir.dt.int32
            MAGIC = 0x5f3759df

            def rsqrt_v(q, dst, src, scr, deps):
                t = q.do(lambda e, d=dst, s=src: e.tensor_scalar(
                    d.bitcast(I32), s.bitcast(I32), 1, None,
                    ALU.logical_shift_right), sig=sV if q is qV else sP,
                    deps=deps)
                sig = sV if q is qV else sP
                t = q.do(lambda e, d=dst: e.tensor_scalar(
                    d.bitcast(I32), d.bitcast(I32), MAGIC, -1,
                    ALU.subtract, ALU.mult), sig=sig, deps=[t])
                for _ in range(3):
                    t = q.do(lambda e, d=dst, sc=scr: e.tensor_mul(
                        sc, d, d), sig=sig, deps=[t])
                    t = q.do(lambda e, s=src, sc=scr: e.tensor_mul(
                        sc, sc, s), sig=sig, deps=[t])
                    t = q.do(lambda e, sc=scr: e.tensor_scalar(
                        sc, sc, -0.5, 1.5, ALU.mult, ALU.add), sig=sig,
                        deps=[t])
                    t = q.do(lambda e, d=dst, sc=scr: e.tensor_mul(
                        d, d, sc), sig=sig, deps=[t])
                return t

            def ln_stats(src_view, two_ln, ci):
                cw = lambda t_: t_[:, ci:ci + 1]
                r = qV.do(lambda v: v.reduce_sum(
                    cw(st_neg), src_view, axis=AX.X, negate=True), sig=sV,
                    deps=[guard.get(f"lnst{ci}")])
                bw(qS, "pQx")
                a = qS.do(lambda s, sv=src_view: s.activation(
                    pQx[:, 0:512], sv[:, 0:512], AF.Square,
                    accum_out=cw(st_ssq)), sig=sS,
                    deps=[guard.get(f"lnsa{ci}")])
                a = qS.do(lambda s, sv=src_view: s.activation(
                    pQx[:, 0:512], sv[:, 512:1024], AF.Square,
                    accum_out=cw(st_a)), sig=sS, deps=[a])
                guard["pQx"] = a
                r = qV.do(lambda v: v.tensor_scalar_mul(
                    cw(st_neg), cw(st_neg), 1.0 / D), sig=sV, deps=[r])
                v1 = qV.do(lambda v: v.tensor_add(
                    cw(st_ssq), cw(st_ssq), cw(st_a)), sig=sV, deps=[a, r])
                v1 = qV.do(lambda v: v.tensor_scalar_mul(
                    cw(st_a), cw(st_ssq), 1.0 / D), sig=sV, deps=[v1])
                v1 = qV.do(lambda v: v.tensor_mul(
                    cw(st_b), cw(st_neg), cw(st_neg)), sig=sV, deps=[v1])
                v1 = qV.do(lambda v: v.tensor_sub(
                    cw(st_a), cw(st_a), cw(st_b)), sig=sV, deps=[v1])
                v1 = qV.do(lambda v: v.tensor_scalar_add(
                    cw(st_b), cw(st_a), EPS), sig=sV, deps=[v1])
                if two_ln:
                    v1 = qV.do(lambda v: v.reciprocal(
                        cw(st_c), cw(st_b)), sig=sV, deps=[v1])
                    v1 = qV.do(lambda v: v.tensor_mul(
                        cw(st_c), cw(st_c), cw(st_a)), sig=sV, deps=[v1])
                    v1 = qV.do(lambda v: v.tensor_scalar_add(
                        cw(st_c), cw(st_c), EPS), sig=sV, deps=[v1])
                    v1 = qV.do(lambda v: v.tensor_mul(
                        cw(st_c), cw(st_c), cw(st_b)), sig=sV, deps=[v1])
                    src = st_c
                else:
                    src = st_b
                a2 = qS.do(lambda s, sr=src: s.activation(
                    cw(st_d), cw(sr), AF.Sqrt), sig=sS, deps=[v1])
                v1 = qV.do(lambda v: v.reciprocal(
                    cw(st_c), cw(st_d)), sig=sV, deps=[a2])
                last_act = a2
                if two_ln:
                    a3 = qS.do(lambda s: s.activation(
                        cw(st_d), cw(st_b), AF.Sqrt), sig=sS, deps=[v1])
                    v1 = qV.do(lambda v: v.reciprocal(
                        cw(st_b), cw(st_d)), sig=sV, deps=[a3])
                    last_act = a3
                guard[f"lnsa{ci}"] = last_act
                guard[f"lnst{ci}"] = v1
                return v1

            def transpose_to(dst_view_fn, src_view_fn, nblk, deps, evac_deps=()):
                bw(qT, "pT")
                t0 = None
                for b in range(nblk):
                    t0 = qT.do(lambda t, b=b: t.transpose(
                        pT[:, b * C:(b + 1) * C], src_view_fn(b), iden_s[:]),
                        sig=sT if b == nblk - 1 else None,
                        deps=deps if b == 0 else ())
                ev = qV.do(lambda v: v.tensor_copy(
                    dst_view_fn(), pT[:, 0:nblk * C]), sig=sV,
                    deps=[t0] + list(evac_deps))
                guard["pT"] = ev
                return ev

            def ln_block(x_view_fn, two_ln, nchunks, hT_dst, tapname,
                         tapbase, loader=None, src_tok=None, pre=None):
                stoks = dict(pre) if pre else {}
                xviews = {}

                def do_stats(c):
                    if c not in stoks:
                        if loader:
                            lt = loader(c)
                            qV.wait(lt)
                            qS.wait(lt)
                        if src_tok is not None:
                            qV.wait(src_tok)
                            qS.wait(src_tok)
                        stoks[c] = ln_stats(x_view_fn(c), two_ln, c)
                    xviews[c] = x_view_fn(c)

                nup = nchunks if loader is None else min(2, nchunks)
                for c in range(nup):
                    do_stats(c)
                for c in range(nchunks):
                    xc = xviews[c]
                    stok = stoks[c]
                    cwn = st_neg[:, c:c + 1]
                    cwc = st_c[:, c:c + 1]
                    qh, sh = (qP, sP) if c % 2 == 0 else (qV, sV)
                    tb, tg = (tscr, "tscr") if c % 2 == 0 else (tscr2, "tscr2")
                    h1 = None
                    for hh in range(2):
                        h1 = qV.do(lambda v, xc=xc, hh=hh, cwn=cwn, cwc=cwc,
                                   tb=tb: v.tensor_scalar(
                                       tb[:], xc[:, hh * 512:hh * 512 + 512],
                                       cwn, cwc, ALU.add, ALU.mult), sig=sV,
                                   deps=([stok] if hh == 0 else [h1])
                                   + [guard.get(tg)])
                        h1 = qh.do(lambda g, hh=hh, tb=tb: g.tensor_mul(
                            tb[:], tb[:], wtile[:, hh * 512:hh * 512 + 512]),
                            sig=sh, deps=[h1])
                        h1 = qh.do(lambda g, hh=hh, tb=tb: g.tensor_add(
                            htok[:, hh * 512:hh * 512 + 512], tb[:],
                            btile[:, hh * 512:hh * 512 + 512]), sig=sh,
                            deps=[h1, guard.get("htok")])
                        guard[tg] = h1
                    ev = transpose_to(
                        lambda c=c: hT_dst[:].rearrange(
                            "p (k t) -> p k t", k=8)[:, :, c * C:(c + 1) * C],
                        lambda b: htok[:, b * C:(b + 1) * C], 8, [h1])
                    guard["htok"] = sT.tok
                    if two_ln:
                        wtok = qV.do(lambda v, xc=xc, c=c, cwn=cwn:
                                     v.tensor_scalar(
                                         xs[:, c * D:(c + 1) * D], xc, cwn,
                                         st_b[:, c:c + 1], ALU.add, ALU.mult),
                                     sig=sV,
                                     deps=[stok,
                                           guard.get(f"xs_store{c}")])
                        guard[f"xc{c % 2}"] = wtok
                        guard[f"xcv{c % 2}"] = wtok
                    if nup <= c + 1 < nchunks:
                        do_stats(c + 1)
                    tap(tapname, htok[:], 128, tapbase + c, [h1], D)

            for st in range(NST1 if (phases & 1) else 0):
                n0 = st * NC1
                def xload(c, n0=n0):
                    return qS.do(lambda s, c=c, n0=n0: s.dma_start(
                        out=xc_buf[c % 2][:],
                        in_=xin_h.rearrange("(n p) f -> p n f", p=128)[
                            :, n0 + c, :]),
                        sig=(dX if c % 2 == 0 else dA),
                        deps=consts_ready + [guard.get(f"xc{c % 2}"),
                                             guard.get(f"xcv{c % 2}")])

                if P1L >= 1:
                    ln_block(lambda c: xc_buf[c % 2][:], True, NC1,
                             hT, "h", st * NC1, loader=xload,
                             pre=pre_ln1.pop(st, None))
                    hT_all = [sV.tok]

                if P1L >= 2:
                    bw(qT, "pQx")
                    mm = None
                    for k in range(8):
                        mm = qT.do(lambda t, k=k: t.matmul(
                            pQx[0:32, :], gk1_s[:].rearrange(
                                "p (k q) -> p k q", k=8)[:, k, :],
                            hT[:, k * ST1:(k + 1) * ST1], start=(k == 0),
                            stop=False),
                            deps=(hT_all + [attnw_ready]) if k == 0 else ())
                    mm = qT.do(lambda t: t.matmul(
                        pQx[0:32, :], e16c_s[:], onesr_s[:], start=False,
                        stop=True), sig=sT)
                    ev1 = qV.do(lambda v: v.tensor_copy(t1ext[:], pQx[0:32, :]),
                                sig=sV, deps=[mm])
                    guard["pQx"] = ev1
                    for c in range(NC1):
                        pb = pP[c % 2]
                        bw(qT, f"pP{c % 2}")
                        mm = qT.do(lambda t, c=c, pb=pb: t.matmul(
                            pb[:, 0:QF], t1ext[:, c * C:(c + 1) * C], gk2_s[:],
                            start=True, stop=True), sig=sT, deps=[ev1])
                        a1 = qS.do(lambda s, pb=pb: s.activation(
                            esb[:, 0:QF], pb[:, 0:QF], AF.Exp, scale=-1.0), sig=sS,
                            deps=[mm, guard.get("esb")])
                        guard[f"pP{c % 2}"] = a1
                        a1 = qS.do(lambda s, c=c: s.activation(
                            gktok[:, c * QF:(c + 1) * QF], esb[:, 0:QF], AF.Ln,
                            bias=cst1[:, 0:1]), sig=sS, deps=[a1])
                        guard["esb"] = a1
                        tap("gk", gktok[:, c * QF:(c + 1) * QF], 128,
                            st * NC1 + c, [a1], QF)
                    gk_done = sS.tok

                if P1L >= 3:
                    for (wbuf, dst) in ((wq_s, qTf), (wk_s, kTf)):
                        for c in range(NC1):
                            pb = pP[c % 2]
                            bw(qT, f"pP{c % 2}")
                            mm = None
                            for k in range(8):
                                mm = qT.do(
                                    lambda t, k=k, c=c, wbuf=wbuf, pb=pb:
                                    t.matmul(
                                        pb[:, 0:512],
                                        hT[:].rearrange(
                                            "p (k t) -> p k t", k=8)[
                                            :, k, c * C:(c + 1) * C],
                                        wbuf[:].rearrange(
                                            "p (k n) -> p k n", k=8)[:, k, :],
                                        start=(k == 0), stop=(k == 7)),
                                    sig=sT if k == 7 else None,
                                    deps=hT_all if k == 0 else ())
                            ev = qV.do(lambda v, c=c, dst=dst, pb=pb:
                                       v.tensor_copy(
                                           dst[:, c * 512:(c + 1) * 512],
                                           pb[:, 0:512]), sig=sV,
                                       deps=[mm, guard.get("qk_w")])
                            guard[f"pP{c % 2}"] = ev
                    qk_done = sV.tok
                    tap("qT", qTf[:, 0:512], 128, st, [qk_done], 512)

                if P1L >= 5:
                    vgtok = {}

                    def vg_proj(c):
                        par = c % 2
                        toks = []
                        for (wbuf, is_g) in ((wv_s, False), (wgg_s, True)):
                            for hf in range(2):
                                pb = pP[hf]
                                bw(qT, f"pP{hf}")
                                mm = None
                                for k in range(8):
                                    mm = qT.do(
                                        lambda t, k=k, c=c, hf=hf, wbuf=wbuf,
                                        pb=pb: t.matmul(
                                            pb[:, 0:512],
                                            hT[:].rearrange(
                                                "p (k t) -> p k t", k=8)[
                                                :, k, c * C:(c + 1) * C],
                                            wbuf[:].rearrange(
                                                "p (k n) -> p k n", k=8)[
                                                :, k,
                                                hf * 512:hf * 512 + 512],
                                            start=(k == 0), stop=(k == 7)),
                                        sig=sT if k == 7 else None,
                                        deps=hT_all if k == 0 else ())
                                col = hf * 512
                                if is_g:
                                    evg = qS.do(
                                        lambda s, col=col, pb=pb, par=par:
                                        s.activation(
                                            gs[par][:, col:col + 512],
                                            pb[:, 0:512], AF.Silu), sig=sS,
                                        deps=[mm, guard.get(f"gs{par}")])
                                    evg = qP.do(
                                        lambda g, col=col, hf=hf, par=par:
                                        g.tensor_mul(
                                            gs[par][:, col:col + 512],
                                            gs[par][:, col:col + 512],
                                            rmstile[:, hf * 512:
                                                    hf * 512 + 512]),
                                        sig=sP, deps=[evg])
                                    guard[f"pP{hf}"] = evg
                                    toks.append(evg)
                                else:
                                    ev = qV.do(
                                        lambda v, col=col, pb=pb, par=par:
                                        v.tensor_copy(
                                            vTok[par][:, col:col + 512],
                                            pb[:, 0:512]), sig=sV,
                                        deps=[mm])
                                    guard[f"pP{hf}"] = ev
                                    toks.append(ev)
                        vgtok[c] = toks

                    def gla_prep(c):
                        par = c % 2
                        bw(qT, "pAT")
                        mm = qT.do(lambda t, c=c: t.matmul(
                            pAT[:, 0:512], um_s[:],
                            gktok[:, c * QF:(c + 1) * QF],
                            start=True, stop=True), sig=sT,
                            deps=[gk_done])
                        a1 = qS.do(lambda s: s.activation(
                            eq_c[:], pAT[:, 0:512], AF.Exp), sig=sS,
                            deps=[mm, guard.get("eqek")])
                        a1 = qS.do(lambda s: s.activation(
                            ek_c[:], pAT[:, 0:512], AF.Exp, scale=-1.0),
                            sig=sS, deps=[a1])
                        exp_done = a1
                        guard["pAT"] = exp_done
                        bw(qT, "pQx")
                        mm = qT.do(lambda t, c=c: t.matmul(
                            pQx[0:1, 0:QF], nsix_s[:],
                            gktok[:, c * QF:(c + 1) * QF],
                            start=True, stop=True), sig=sT,
                            deps=[gk_done])
                        a2 = qS.do(lambda s, c=c: s.activation(
                            gktok[0:1, c * QF:(c + 1) * QF],
                            pQx[0:1, 0:QF], AF.Exp), sig=sS, deps=[mm])
                        mm = None
                        for h in range(HEADS):
                            mm = qT.do(lambda t, h=h, c=c: t.matmul(
                                pQx[0:32, 0:E],
                                gktok[0:1, c * QF + 32 * h:
                                      c * QF + 32 * (h + 1)],
                                e16s_s[0:1, E * h:E * (h + 1)],
                                start=(h == 0), stop=(h == HEADS - 1)),
                                sig=sT if h == HEADS - 1 else None,
                                deps=[a2] if h == 0 else ())
                        dv1 = qV.do(lambda v, par=par: v.tensor_copy(
                            dec_sb[par][:], pQx[0:32, 0:E]), sig=sV,
                            deps=[mm, guard.get(f"dec_p{par}")])
                        mm = qT.do(lambda t, c=c: t.matmul(
                            pQx[:, 0:QF], ones_s[0:1, :],
                            gktok[0:1, c * QF:(c + 1) * QF],
                            start=True, stop=True), sig=sT, deps=[dv1])
                        eklb = mm
                        v3 = qV.do(lambda v, c=c: v.scalar_tensor_tensor(
                            qTf[:, c * 512:(c + 1) * 512],
                            qTf[:, c * 512:(c + 1) * 512],
                            DKS, eq_c[:], ALU.mult, ALU.mult), sig=sV,
                            deps=[exp_done, qk_done])
                        v3 = qV.do(lambda v, c=c: v.tensor_tensor(
                            kTf[:, c * 512:(c + 1) * 512],
                            kTf[:, c * 512:(c + 1) * 512],
                            ek_c[:], ALU.mult), sig=sV, deps=[v3])
                        v3 = qV.do(lambda v, c=c, par=par: v.tensor_tensor(
                            kdt_c[par][:], kTf[:, c * 512:(c + 1) * 512],
                            pQx[:, 0:QF], ALU.mult), sig=sV,
                            deps=[v3, eklb])
                        guard["pQx"] = v3
                        guard["eqek"] = v3
                        return v3, dv1

                    vg_proj(0)
                    prep = gla_prep(0)
                    for c in range(NC1):
                        glob_c = st * NC1 + c
                        par = c % 2
                        qkg_done, dv1 = prep
                        bw(qT, "pO")
                        o_first = []
                        for w in range(2):
                            for (src, dsth) in ((qTf, qgT_hf),
                                                (kTf, kgT_hf)):
                                bw(qT, "pT")
                                t0 = None
                                for i in range(8):
                                    h = w * 8 + i
                                    t0 = qT.do(
                                        lambda t, i=i, h=h, c=c, src=src:
                                        t.transpose(
                                            pT[0:32, i * C:(i + 1) * C],
                                            src[:, c * 512 + 32 * h:
                                                c * 512 + 32 * (h + 1)],
                                            iden_s[:]),
                                        sig=sT if i == 7 else None,
                                        deps=[qkg_done] if i == 0 else ())
                                ev = qV.do(lambda v, dsth=dsth:
                                           v.tensor_copy(
                                               dsth[:],
                                               pT[0:32, 0:8 * C]),
                                           sig=sV, deps=[t0])
                                guard["pT"] = ev
                            tr_ev = ev
                            bw(qT, "pAT")
                            mm = None
                            for i in range(8):
                                mm = qT.do(lambda t, i=i: t.matmul(
                                    pAT[:, i * C:(i + 1) * C],
                                    kgT_hf[0:32, i * C:(i + 1) * C],
                                    qgT_hf[0:32, i * C:(i + 1) * C],
                                    start=True, stop=True),
                                    sig=sT if i == 7 else None,
                                    deps=[tr_ev] if i == 0 else ())
                            atm_ev = qV.do(lambda v, w=w: v.tensor_tensor(
                                atm_c[:].rearrange(
                                    "p (h t) -> p h t", h=HEADS)[
                                    :, w * 8:(w + 1) * 8, :],
                                pAT[:].rearrange("p (h t) -> p h t", h=8),
                                _bc(mAT_s[:, None, :], [128, 8, C]),
                                ALU.mult), sig=sV, deps=[mm])
                            guard["pAT"] = atm_ev
                            mm = None
                            for i in range(8):
                                h = w * 8 + i
                                dep0 = ([atm_ev, guard.get("S_bf")]
                                        + vgtok[c]) if i == 0 else []
                                qT.do(lambda t, i=i, h=h: t.matmul(
                                    pO[:, h * DV:(h + 1) * DV],
                                    qgT_hf[0:32, i * C:(i + 1) * C],
                                    S_bf[0:32, h * DV:(h + 1) * DV],
                                    start=True, stop=False), deps=dep0)
                                mm = qT.do(lambda t, h=h, c=c: t.matmul(
                                    pO[:, h * DV:(h + 1) * DV],
                                    atm_c[:, h * C:(h + 1) * C],
                                    vTok[c % 2][:, h * DV:(h + 1) * DV],
                                    start=False, stop=True),
                                    sig=sT if i == 7 else None)
                            o_first.append(mm)
                        o_done = o_first[-1]
                        if c + 1 < NC1:
                            vg_proj(c + 1)
                            prep = gla_prep(c + 1)
                        bw(qS, "pAT")
                        a3 = qS.do(lambda s: s.activation(
                            pAT[:, 0:1024], pO[:, 0:1024], AF.Square),
                            sig=sS, deps=[o_done])
                        v4 = qV.do(lambda v: v.reduce_sum(
                            ss16[:], pAT[:].rearrange(
                                "p (h v) -> p h v", h=HEADS),
                            axis=AX.X), sig=sV,
                            deps=[a3, guard.get("ss_a")])
                        guard["pAT"] = v4
                        v4 = qV.do(lambda v: v.tensor_scalar(
                            ss16[:], ss16[:], 1.0 / DV, EPS,
                            ALU.mult, ALU.add), sig=sV, deps=[v4])
                        a4 = qS.do(lambda s: s.activation(
                            rg16[:], ss16[:], AF.Sqrt), sig=sS, deps=[v4])
                        guard["ss_a"] = a4
                        v4 = qV.do(lambda v: v.reciprocal(
                            ri16[:], rg16[:]), sig=sV, deps=[a4])
                        og = atm_c[:, 0:D]
                        v5 = qV.do(lambda v, c=c: v.tensor_tensor(
                            og, pO[:, 0:D], gs[c % 2][:],
                            ALU.mult), sig=sV, deps=[v4] + vgtok[c])
                        guard[f"gs{c % 2}"] = v5
                        v5 = qV.do(lambda v: v.tensor_tensor(
                            og.rearrange("p (h v) -> p h v", h=HEADS),
                            og.rearrange("p (h v) -> p h v", h=HEADS),
                            _bc(ri16[:, :, None], [128, HEADS, DV]),
                            ALU.mult), sig=sV, deps=[v5])
                        og_done = v5
                        tap("og", og, 128, glob_c, [og_done], D)
                        bw(qT, "pO")
                        mm = None
                        for h in range(HEADS):
                            mm = qT.do(lambda t, h=h, c=c, par=par: t.matmul(
                                pO[0:32, h * DV:(h + 1) * DV],
                                kdt_c[par][:, 32 * h:32 * (h + 1)],
                                vTok[c % 2][:, h * DV:(h + 1) * DV],
                                start=True, stop=True),
                                sig=sT if h == HEADS - 1 else None,
                                deps=[og_done] if h == 0 else ())
                        sup = mm
                        p1 = qP.do(lambda g, par=par: g.tensor_tensor(
                            S_sb[:].rearrange("p (h v) -> p h v", h=HEADS),
                            S_sb[:].rearrange("p (h v) -> p h v", h=HEADS),
                            _bc(dec_sb[par][:, :, None], [32, HEADS, DV]),
                            ALU.mult), sig=sP, deps=[dv1, og_done])
                        guard[f"dec_p{par}"] = p1
                        v6 = qV.do(lambda v: v.tensor_add(
                            S_sb[:], S_sb[:], pO[0:32, :]), sig=sV,
                            deps=[sup, p1])
                        v6 = qV.do(lambda v: v.tensor_copy(
                            S_bf[:], S_sb[:]), sig=sV, deps=[v6])
                        guard["pO"] = v6
                        guard["S_bf"] = v6
                        ogt_ev = transpose_to(
                            lambda: ogT_c[:],
                            lambda b: og[:, b * C:(b + 1) * C], 8,
                            [og_done])
                        for hf in range(2):
                            pb = pP[hf]
                            bw(qT, f"pP{hf}")
                            mm = None
                            for fc in range(8):
                                mm = qT.do(
                                    lambda t, fc=fc, hf=hf, pb=pb:
                                    t.matmul(
                                        pb[:, 0:512],
                                        ogT_c[:, fc * C:(fc + 1) * C],
                                        wo_s[:].rearrange(
                                            "p (k n) -> p k n", k=8)[
                                            :, fc,
                                            hf * 512:hf * 512 + 512],
                                        start=(fc == 0), stop=(fc == 7)),
                                    sig=sT if fc == 7 else None,
                                    deps=[ogt_ev] if fc == 0 else ())
                            ev = qV.do(lambda v, c=c, hf=hf, pb=pb:
                                       v.scalar_tensor_tensor(
                                           xs[:, c * D + hf * 512:
                                              c * D + hf * 512 + 512],
                                           pb[:, 0:512], 1.0,
                                           xs[:, c * D + hf * 512:
                                              c * D + hf * 512 + 512],
                                           ALU.mult, ALU.add), sig=sV,
                                       deps=[mm])
                            guard[f"pP{hf}"] = ev
                    xb_done = sV.tok
                    tap("xB", xs[:, 0:D], 128, st, [xb_done], D)
                if P1L >= 6:
                    ln_block(lambda c: xs[:, c * D:(c + 1) * D], False, NC1,
                             hT, "h2", st * NC1, src_tok=xb_done)
                    h2_all = [sV.tok]
                    ln2_stats_done = sS.tok
                    addt = {}
                    for c in (2, 3):
                        addt[c] = qV.do(lambda v, c=c: v.tensor_add(
                            xs[:, c * D:(c + 1) * D], xs[:, c * D:(c + 1) * D],
                            xc_buf[c % 2][:]), sig=sV, deps=[ln2_stats_done])
                    for c in (0, 1):
                        rlt = qS.do(lambda s, c=c, n0=n0: s.dma_start(
                            out=xc_buf[c % 2][:],
                            in_=xin_h.rearrange("(n p) f -> p n f", p=128)[
                                :, n0 + c, :]), sig=(dX if c % 2 == 0 else dA),
                            deps=[addt[c + 2]])
                        addt[c] = qV.do(lambda v, c=c: v.tensor_add(
                            xs[:, c * D:(c + 1) * D], xs[:, c * D:(c + 1) * D],
                            xc_buf[c % 2][:]), sig=sV, deps=[rlt])
                    guard["xc0"] = guard["xcv0"] = addt[0]
                    guard["xc1"] = guard["xcv1"] = addt[1]
                    qV.wait(addt[1])
                    if (phases & 2) and st >= 1 and st - 1 < E:
                        e = st - 1
                        pf_tok[e] = [qP.do(
                            lambda g, d=dc, s=sf, e=e, f=f_: g.dma_start(
                                out=d[e].rearrange(
                                    "p (m n) -> p m n", m=f),
                                in_=s[e].rearrange(
                                    "(m p) n -> p m n", p=128)),
                            sig=eSs[e % 3][j])
                            for j, (sf, dc, f_) in enumerate(
                                ((ewg_h, ewgc_h, 8), (ewu_h, ewuc_h, 8),
                                 (ewd_h, ewdc_h, 4)))]

                if P1L >= 7:
                    w2ev = {}
                    for qu in range(4):
                        wb = w1b[qu % 2]
                        gq = gelu_q[qu % 2]
                        if st == 0:
                            lt = cast_load(
                                wb[:].rearrange("p (k n) -> p k n", k=8),
                                w1_h.rearrange("(k p) n -> p k n", p=128)[
                                    :, :, qu * 512:(qu + 1) * 512], w1s[qu % 2],
                                deps=[guard.get(f"w1b{qu % 2}"),
                                      guard.get(f"w1cw{qu % 2}")])
                            cw = qS.do(lambda s, qu=qu, wb=wb: s.dma_start(
                                out=w1c_h[qu], in_=wb[:]),
                                sig=dCW, deps=[lt])
                            guard[f"w1cw{qu % 2}"] = cw
                        else:
                            lt = qY.do(lambda y, qu=qu, wb=wb: y.dma_start(
                                out=wb[:], in_=w1c_h[qu]),
                                sig=w1y[qu % 2],
                                deps=[guard.get(f"w1b{qu % 2}"),
                                      guard.get(f"w1cw{qu % 2}"), dCW.tok])
                        for uc in range(4):
                            pb = pP[uc % 2]
                            bw(qT, f"pP{uc % 2}")
                            mm = None
                            for k in range(8):
                                mm = qT.do(lambda t, k=k, uc=uc, wb=wb, pb=pb:
                                           t.matmul(
                                               pb[:, 0:ST1],
                                               wb[:].rearrange(
                                                   "p (k u q) -> p k u q",
                                                   k=8, u=4)[:, k, uc, :],
                                               hT[:, k * ST1:(k + 1) * ST1],
                                               start=(k == 0), stop=(k == 7)),
                                           sig=sT if k == 7 else None,
                                           deps=(h2_all + [lt]) if k == 0 else ())
                            ev = qS.do(lambda s, u=qu * 4 + uc, uc=uc, pb=pb,
                                       gq=gq: s.activation(
                                gq[:, uc * ST1:(uc + 1) * ST1], pb[:, 0:ST1],
                                AF.Gelu, bias=b1col[:, u:u + 1]), sig=sS,
                                deps=[mm, guard.get(f"gelu{qu % 2}")])
                            guard[f"pP{uc % 2}"] = ev
                        guard[f"w1b{qu % 2}"] = sT.tok
                        gelu_done = sS.tok
                        if qu == 0 and "gelu" in taps:
                            tap("gelu", gq[:, 0:ST1], 128, st, [gelu_done], ST1)
                        wb2 = w2b[qu % 2]
                        if st == 0:
                            lt2 = cast_load(
                                wb2[:].rearrange("p (u n) -> p u n", u=4),
                                w2_h.rearrange("(u p) n -> p u n", p=128)[
                                    :, qu * 4:(qu + 1) * 4, :], w2s[qu % 2],
                                deps=[guard.get(f"w2b{qu % 2}"),
                                      guard.get(f"w2cw{qu % 2}")])
                            cw = qS.do(lambda s, qu=qu, wb2=wb2: s.dma_start(
                                out=w2c_h[qu], in_=wb2[:]),
                                sig=dCW, deps=[lt2])
                            guard[f"w2cw{qu % 2}"] = cw
                        else:
                            lt2 = qY.do(lambda y, qu=qu, wb2=wb2: y.dma_start(
                                out=wb2[:], in_=w2c_h[qu]),
                                sig=w2y[qu % 2],
                                deps=[guard.get(f"w2b{qu % 2}"),
                                      guard.get(f"w2cw{qu % 2}"), dCW.tok])
                        for c in range(NC1):
                            for hf in range(2):
                                pb = pP[(c * 2 + hf) % 2]
                                bw(qT, f"pP{(c * 2 + hf) % 2}")
                                mm = None
                                for uc in range(4):
                                    last = (uc == 3 and qu != 0)
                                    mm = qT.do(lambda t, uc=uc, c=c, hf=hf,
                                               wb2=wb2, pb=pb, last=last, gq=gq:
                                               t.matmul(
                                                   pb[:, 0:512],
                                                   gq[:].rearrange(
                                                       "p (u t) -> p u t",
                                                       u=4)[:, uc,
                                                            c * C:(c + 1) * C],
                                                   wb2[:].rearrange(
                                                       "p (u h f) -> p u h f",
                                                       u=4, h=2)[:, uc, hf, :],
                                                   start=(uc == 0), stop=last),
                                               sig=sT if last else None,
                                               deps=[gelu_done, lt2]
                                               if uc == 0 else ())
                                if qu == 0:
                                    mm = qT.do(lambda t, hf=hf, pb=pb: t.matmul(
                                        pb[:, 0:512], onesb_s[0:1, :],
                                        b2row[0:1, hf * 512:hf * 512 + 512],
                                        start=False, stop=True), sig=sT)
                                ev = qV.do(lambda v, c=c, hf=hf, pb=pb:
                                           v.scalar_tensor_tensor(
                                               xs[:, c * D + hf * 512:
                                                  c * D + hf * 512 + 512],
                                               pb[:, 0:512], 1.0,
                                               xs[:, c * D + hf * 512:
                                                  c * D + hf * 512 + 512],
                                               ALU.mult, ALU.add), sig=sV,
                                           deps=[mm])
                                guard[f"pP{(c * 2 + hf) % 2}"] = ev
                                if qu == 3 and hf == 1:
                                    w2ev[c] = ev
                        guard[f"w2b{qu % 2}"] = sT.tok
                        guard[f"gelu{qu % 2}"] = sT.tok
                        if qu == 1 and P1L >= 8 and st + 1 < NST1:
                            n1 = (st + 1) * NC1
                            pd = {}
                            for c in (0, 1):
                                lt = qS.do(lambda s, c=c, n1=n1: s.dma_start(
                                    out=xc_buf[c][:],
                                    in_=xin_h.rearrange(
                                        "(n p) f -> p n f", p=128)[
                                        :, n1 + c, :]),
                                    sig=(dX2 if c == 0 else dA2),
                                    deps=[guard.get(f"xc{c}"),
                                          guard.get(f"xcv{c}")])
                                qV.wait(lt)
                                qS.wait(lt)
                                guard[f"lnst{c}"] = sV.tok
                                guard[f"lnsa{c}"] = sS.tok
                                pd[c] = ln_stats(xc_buf[c][:], True, c)
                            pre_ln1[st + 1] = pd

                if P1L >= 8:
                    for c in range(NC1):
                        stok = qS.do(lambda s, c=c, n0=n0: s.dma_start(
                            out=x1scr.rearrange("(n p) f -> p n f", p=128)[
                                :, n0 + c, :],
                            in_=xs[:, c * D:(c + 1) * D]), sig=dS,
                            deps=[w2ev[c]])
                        x1_stored.append(stok)
                        guard[f"xs_store{c}"] = stok
                    tap("x1", xs[:, 0:D], 128, st, [guard["xs_store0"]], D)

            phase1_end = [sT.tok, sV.tok, sS.tok, sP.tok, dS.tok, dY.tok, dP.tok,
                          dR.tok, dX.tok, dA.tok, dX2.tok, dA2.tok, dCW.tok,
                          w1s[0].tok, w1s[1].tok, w2s[0].tok, w2s[1].tok]
            flush()

        nsub_ = ST2_ // 512
        es2 = ExitStack()
        with es2:
            x1t_ = es2.enter_context(
                nc.sbuf_tensor("x1t", [128, NC2 * D], F32))
            x1t = [x1t_, x1t_]
            x1c = es2.enter_context(nc.sbuf_tensor("x1c", [128, D], F32))
            h3T = [es2.enter_context(
                nc.sbuf_tensor(f"h3T{i}", [128, 8 * ST2_], BF16))
                for i in range(2)]
            route = [es2.enter_context(
                nc.sbuf_tensor(f"route{i}", [128, NC2 * E], F32))
                for i in range(2)]
            scores = es2.enter_context(nc.sbuf_tensor("scores", [128, E], F32))
            masked = es2.enter_context(nc.sbuf_tensor("masked", [128, E], F32))
            mx8 = es2.enter_context(nc.sbuf_tensor("mx8", [128, 8], F32))
            mrscr = es2.enter_context(nc.sbuf_tensor("mrscr", [128, 8], F32))
            w12 = es2.enter_context(nc.sbuf_tensor("w12", [128, 2], F32))
            eqv = es2.enter_context(nc.sbuf_tensor("eqv", [128, E], F32))
            indv = es2.enter_context(nc.sbuf_tensor("indv", [128, E], F32))
            p_sb = [es2.enter_context(
                nc.sbuf_tensor(f"p_sb{i}", [128, nsub_ * 4 * 512], BF16))
                for i in range(2)]
            sil_sb = [es2.enter_context(
                nc.sbuf_tensor(f"sil_sb{i}", [128, 2 * 512], BF16))
                for i in range(2)]
            gwt_s = es2.enter_context(nc.sbuf_tensor("gwt_s", [128, 8 * E], F32))
            h3f = es2.enter_context(nc.sbuf_tensor("h3f", [128, D], F32))
            h3Tf = es2.enter_context(nc.sbuf_tensor("h3Tf", [128, D], F32))
            swg_s = es2.enter_context(nc.sbuf_tensor("swg_s", [128, 8 * 512], BF16))
            swu_s = es2.enter_context(nc.sbuf_tensor("swu_s", [128, 8 * 512], BF16))
            swd_s = es2.enter_context(nc.sbuf_tensor("swd_s", [128, 4 * 1024], BF16))
            ewgb = [es2.enter_context(
                nc.sbuf_tensor(f"ewgb{i}", [128, 8 * 512], BF16))
                for i in range(3)]
            ewub = [es2.enter_context(
                nc.sbuf_tensor(f"ewub{i}", [128, 8 * 512], BF16))
                for i in range(3)]
            ewdb = [es2.enter_context(
                nc.sbuf_tensor(f"ewdb{i}", [128, 4 * 1024], BF16))
                for i in range(3)]
            s2_neg = es2.enter_context(nc.sbuf_tensor("s2_neg", [128, 1], F32))
            s2_ssq = es2.enter_context(nc.sbuf_tensor("s2_ssq", [128, 1], F32))
            s2_a = es2.enter_context(nc.sbuf_tensor("s2_a", [128, 1], F32))
            s2_b = es2.enter_context(nc.sbuf_tensor("s2_b", [128, 1], F32))
            s2_c2 = es2.enter_context(nc.sbuf_tensor("s2_c2", [128, 1], F32))

            qP.do(lambda g: g.dma_start(
                out=gwt_s[:].rearrange("p (k e) -> p k e", k=8),
                in_=gwt_h.rearrange("(k p) e -> p k e", p=128)),
                sig=dP, deps=phase1_end)
            qP.do(lambda g: g.dma_start(
                out=swg_s[:].rearrange("p (k n) -> p k n", k=8),
                in_=swg_h.rearrange("(k p) n -> p k n", p=128)),
                sig=dP)
            qP.do(lambda g: g.dma_start(
                out=swu_s[:].rearrange("p (k n) -> p k n", k=8),
                in_=swu_h.rearrange("(k p) n -> p k n", p=128)),
                sig=dP)
            qP.do(lambda g: g.dma_start(
                out=swd_s[:].rearrange("p (m n) -> p m n", m=4),
                in_=swd_h.rearrange("(m p) n -> p m n", p=128)),
                sig=dP)
            p2w_ready = dP.tok

            cw_tok = {}

            def exp_load(e, bi, first_pass, deps=()):
                toks = list(deps)
                out_toks = []
                for j, (bufs, src_f, src_c, kind) in enumerate((
                    (ewgb, ewg_h, ewgc_h, "g"), (ewub, ewu_h, ewuc_h, "g"),
                    (ewdb, ewd_h, ewdc_h, "d"),
                )):
                    dst = bufs[bi]
                    if kind == "g":
                        dv = dst[:].rearrange("p (k n) -> p k n", k=8)
                        sf = src_f[e].rearrange("(k p) n -> p k n", p=128)
                    else:
                        dv = dst[:].rearrange("p (m n) -> p m n", m=4)
                        sf = src_f[e].rearrange("(m p) n -> p m n", p=128)
                    if e in pf_tok and not EXP_NOCACHE:
                        t = qY.do(lambda y, d=dst, s=src_c, e=e: y.dma_start(
                            out=d[:], in_=s[e]), sig=eYs[bi][j],
                            deps=toks + [pf_tok[e][j]])
                    else:
                        t = qP.do(lambda g, d=dv, s=sf: g.dma_start(
                            out=d, in_=s), sig=eSs[bi][j], deps=toks)
                    out_toks.append(t)
                return out_toks

            eload = {(0, 0): (exp_load(0, 0, True, deps=phase1_end), 0),
                     (0, 1): (exp_load(1, 1, True, deps=phase1_end), 1)}

            pre_tok = {}

            def prelude_chunk(st, c):
                b = st % 2
                n0 = st * (ST2_ // 128)
                if True:
                    xc = x1c[:]
                    xlc = qS.do(lambda s, n0=n0, c=c: s.dma_start(
                        out=x1c[:],
                        in_=x1scr.rearrange("(n p) f -> p n f", p=128)[
                            :, n0 + c, :]), sig=dX,
                        deps=[guard.get("x1c")] + phase1_end)
                    qV.wait(xlc)
                    qS.wait(xlc)
                    r = qV.do(lambda v, xc=xc: v.reduce_sum(
                        s2_neg[:], xc, axis=AX.X, negate=True), sig=sV,
                        deps=[guard.get("ln2g")])
                    r = qV.do(lambda v: v.tensor_scalar_mul(
                        s2_neg[:], s2_neg[:], 1.0 / D), sig=sV, deps=[r])
                    bw(qS, "pQx")
                    a = qS.do(lambda s, xc=xc: s.activation(
                        pQx[:, 0:512], xc[:, 0:512], AF.Square,
                        bias=s2_neg[:, 0:1], accum_out=s2_ssq[:]), sig=sS,
                        deps=[r])
                    a = qS.do(lambda s, xc=xc: s.activation(
                        pQx[:, 0:512], xc[:, 512:1024], AF.Square,
                        bias=s2_neg[:, 0:1], accum_out=s2_a[:]), sig=sS,
                        deps=[a])
                    guard["pQx"] = a
                    r = qV.do(lambda v: v.tensor_add(
                        s2_ssq[:], s2_ssq[:], s2_a[:]), sig=sV, deps=[a])
                    r = qV.do(lambda v: v.tensor_scalar(
                        s2_b[:], s2_ssq[:], 1.0 / D, EPS, ALU.mult, ALU.add),
                        sig=sV, deps=[r])
                    a = qS.do(lambda s: s.activation(
                        s2_c2[:], s2_b[:], AF.Sqrt), sig=sS, deps=[r])
                    guard["ln2g"] = a
                    r = qV.do(lambda v: v.reciprocal(s2_b[:], s2_c2[:]),
                              sig=sV, deps=[a])
                    h1 = qV.do(lambda v, xc=xc: v.tensor_scalar(
                        h3f[:], xc, s2_neg[:, 0:1], s2_b[:, 0:1],
                        ALU.add, ALU.mult), sig=sV,
                        deps=[r, guard.get("h3f")])
                    guard["x1c"] = h1
                    tap("h3", h3f[:], 128, st * NC2 + c, [h1], D)
                    for rd in range(2):
                        bw(qT, "pQx")
                        t0 = None
                        for i in range(4):
                            t0 = qT.do(lambda t, i=i, rd=rd: t.transpose(
                                pQx[:, i * C:(i + 1) * C],
                                h3f[:, (rd * 4 + i) * C:
                                    (rd * 4 + i + 1) * C], idenf_s[:]),
                                sig=sT if i == 3 else None,
                                deps=[h1, guard.get("h3Tf"),
                                      guard.get("h3f")] if i == 0 else ())
                        evf = qV.do(lambda v, rd=rd: v.tensor_copy(
                            h3Tf[:, rd * 512:(rd + 1) * 512],
                            pQx[:, 0:512]), sig=sV, deps=[t0])
                        guard["pQx"] = evf
                        guard["h3f"] = sT.tok
                    hv = qV.do(lambda v, c=c, b=b: v.tensor_copy(
                        h3T[b][:].rearrange("p (k t) -> p k t", k=8)[
                            :, :, c * C:(c + 1) * C],
                        h3Tf[:].rearrange("p (k t) -> p k t", k=8)),
                        sig=sV, deps=[evf])
                    bw(qT, "pQx")
                    mm = None
                    for k in range(8):
                        mm = qT.do(lambda t, k=k: t.matmul(
                            pQx[:, 0:E],
                            h3Tf[:, k * C:(k + 1) * C],
                            gwt_s[:].rearrange("p (k e) -> p k e", k=8)[
                                :, k, :],
                            start=(k == 0), stop=(k == 7)),
                            sig=sT if k == 7 else None,
                            deps=[evf, p2w_ready] if k == 0 else ())
                    guard["h3Tf"] = [mm, hv]
                    a = qS.do(lambda s: s.activation(
                        scores[:], pQx[:, 0:E], AF.Sigmoid), sig=sS,
                        deps=[mm, guard.get("topk")])
                    guard["pQx"] = a
                    v1 = qV.do(lambda v: v.max(out=mx8[:], in_=scores[:]),
                               sig=sV, deps=[a])
                    v1 = qV.do(lambda v: v.tensor_copy(
                        mrscr[:, 0:2], mx8[:, 0:2]), sig=sV, deps=[v1])
                    v1 = qV.do(lambda v: v.memset(mrscr[:, 2:8], -1.0),
                               sig=sV, deps=[v1])
                    v1 = qV.do(lambda v: v.match_replace(
                        out=masked[:], in_to_replace=mrscr[:],
                        in_values=scores[:], imm_value=-2.0), sig=sV,
                        deps=[v1])
                    v1 = qV.do(lambda v: v.tensor_tensor(
                        indv[:], scores[:], masked[:], ALU.not_equal),
                        sig=sV, deps=[v1])
                    v1 = qV.do(lambda v: v.tensor_scalar(
                        eqv[:], scores[:], mx8[:, 0:1], None, ALU.is_equal),
                        sig=sV, deps=[v1])
                    v1 = qV.do(lambda v: v.tensor_add(
                        w12[:, 0:1], mx8[:, 0:1], mx8[:, 1:2]), sig=sV,
                        deps=[v1])
                    v1 = qV.do(lambda v: v.tensor_scalar_add(
                        w12[:, 0:1], w12[:, 0:1], 1e-20), sig=sV, deps=[v1])
                    v1 = qV.do(lambda v: v.reciprocal(
                        w12[:, 1:2], w12[:, 0:1]), sig=sV, deps=[v1])
                    v1 = qV.do(lambda v: v.tensor_scalar(
                        w12[:, 0:2], mx8[:, 0:2], w12[:, 1:2], 2.5,
                        ALU.mult, ALU.mult), sig=sV, deps=[v1])
                    v1 = qV.do(lambda v: v.tensor_sub(
                        indv[:], indv[:], eqv[:]), sig=sV, deps=[v1])
                    v1 = qV.do(lambda v: v.tensor_scalar_mul(
                        indv[:], indv[:], w12[:, 1:2]), sig=sV, deps=[v1])
                    v1 = qV.do(lambda v: v.tensor_scalar_mul(
                        eqv[:], eqv[:], w12[:, 0:1]), sig=sV, deps=[v1])
                    v1 = qV.do(lambda v, c=c, b=b: v.tensor_add(
                        route[b][:, c * E:(c + 1) * E], eqv[:], indv[:]),
                        sig=sV, deps=[v1])
                    guard["topk"] = v1
                    tap("route", route[b][:, c * E:(c + 1) * E], 128,
                        st * NC2 + c, [v1], E)
                pre_tok[(st, c)] = [sV.tok]

            def prelude(st):
                for c in range(NC2):
                    prelude_chunk(st, c)

            def gup(ei, st, ldtoks):
                b = st % 2
                par = ei % 2
                wgb = swg_s if ei == E else ewgb[eb_of[(st, ei)]]
                wub = swu_s if ei == E else ewub[eb_of[(st, ei)]]
                for sb in range(nsub):
                    tcol = sb * 512
                    for mc in range(4):
                        bw(qT, "pP0")
                        mm = None
                        for k in range(8):
                            mm = qT.do(
                                lambda t, k=k, mc=mc, tcol=tcol, wgb=wgb,
                                b=b: t.matmul(
                                    pP0[:, 0:512],
                                    wgb[:].rearrange(
                                        "p (k m q) -> p k m q",
                                        k=8, m=4)[:, k, mc, :],
                                    h3T[b][:, k * ST2_ + tcol:
                                           k * ST2_ + tcol + 512],
                                    start=(k == 0), stop=(k == 7)),
                                sig=sT if k == 7 else None,
                                deps=(pre_tok[(st, NC2 - 1)] + ldtoks)
                                if k == 0 else ())
                        sl = mc % 2
                        sev = qS.do(lambda s, par=par, sl=sl: s.activation(
                            sil_sb[par][:, sl * 512:sl * 512 + 512],
                            pP0[:, 0:512], AF.Silu), sig=sS,
                            deps=[mm, guard.get(f"sil{par}{sl}")])
                        guard["pP0"] = sev
                        bw(qT, "pP1")
                        mm = None
                        for k in range(8):
                            mm = qT.do(
                                lambda t, k=k, mc=mc, tcol=tcol, wub=wub,
                                b=b: t.matmul(
                                    pP1[:, 0:512],
                                    wub[:].rearrange(
                                        "p (k m q) -> p k m q",
                                        k=8, m=4)[:, k, mc, :],
                                    h3T[b][:, k * ST2_ + tcol:
                                           k * ST2_ + tcol + 512],
                                    start=(k == 0), stop=(k == 7)),
                                sig=sT if k == 7 else None)
                        pev = qV.do(lambda v, mc=mc, sb=sb, par=par, sl=sl:
                                    v.tensor_tensor(
                                        p_sb[par][:, (sb * 4 + mc) * 512:
                                                  (sb * 4 + mc + 1) * 512],
                                        pP1[:, 0:512],
                                        sil_sb[par][:, sl * 512:sl * 512 + 512],
                                        ALU.mult), sig=sV, deps=[mm, sev])
                        guard["pP1"] = pev
                        guard[f"sil{par}{sl}"] = pev
                return sV.tok

            def down(ei, st, p_done):
                b = st % 2
                par = ei % 2
                wdb = swd_s if ei == E else ewdb[eb_of[(st, ei)]]
                for sb in range(nsub):
                    for tc in range(4):
                        ch = sb * 4 + tc
                        for hf in range(2):
                            pb = pAT if hf == 0 else pO
                            nm = "pAT" if hf == 0 else "pO"
                            bw(qT, nm)
                            mm = None
                            for mc in range(4):
                                mm = qT.do(
                                    lambda t, mc=mc, ch=ch, hf=hf, pb=pb,
                                    wdb=wdb, par=par: t.matmul(
                                        pb[:, 0:512],
                                        p_sb[par][:].rearrange(
                                            "p (m t) -> p m t",
                                            m=nsub * 4)[
                                            :, (ch // 4) * 4 + mc,
                                            (ch % 4) * C:(ch % 4 + 1) * C],
                                        wdb[:].rearrange(
                                            "p (m h f) -> p m h f",
                                            m=4, h=2)[:, mc, hf, :],
                                        start=(mc == 0), stop=(mc == 3)),
                                    sig=sT if mc == 3 else None,
                                    deps=[p_done] if mc == 0 else ())
                            qV.wait(x1t_ld[st])
                            if ei == E:
                                ev = qV.do(
                                    lambda v, ch=ch, hf=hf, pb=pb, b=b:
                                    v.scalar_tensor_tensor(
                                        x1t[b][:, ch * D + hf * 512:
                                               ch * D + hf * 512 + 512],
                                        pb[:, 0:512], 1.0,
                                        x1t[b][:, ch * D + hf * 512:
                                               ch * D + hf * 512 + 512],
                                        ALU.mult, ALU.add), sig=sV,
                                    deps=[mm])
                            else:
                                ev = qV.do(
                                    lambda v, ch=ch, hf=hf, pb=pb, ei=ei,
                                    b=b: v.scalar_tensor_tensor(
                                        x1t[b][:, ch * D + hf * 512:
                                               ch * D + hf * 512 + 512],
                                        pb[:, 0:512],
                                        route[b][:, ch * E + ei:
                                                 ch * E + ei + 1],
                                        x1t[b][:, ch * D + hf * 512:
                                               ch * D + hf * 512 + 512],
                                        ALU.mult, ALU.add), sig=sV,
                                    deps=[mm])
                            guard[nm] = ev
                if ei != E:
                    guard[f"eb{eb_of[(st, ei)]}"] = sT.tok

            nsub = ST2_ // 512
            eb_of = {}
            x1t_ld = {}
            prelude(0)
            for st in range(NST2):
                n0_ = st * (ST2_ // 128)
                x1t_ld[st] = qS.do(lambda s, n0=n0_: s.dma_start(
                    out=x1t_[:].rearrange("p (n f) -> p n f", n=NC2),
                    in_=x1scr.rearrange("(n p) f -> p n f", p=128)[
                        :, n0:n0 + NC2, :]), sig=dS,
                    deps=[guard.get("x1t")] + phase1_end)
                pend = None
                for ei in range(E + 1):
                    if ei != E:
                        ldtoks, bi = eload.pop((st, ei))
                        eb_of[(st, ei)] = bi
                    else:
                        ldtoks = [p2w_ready]
                    pt = gup(ei, st, ldtoks)
                    if pend is not None:
                        down(pend[0], st, pend[1])
                    if ei != E:
                        gi = st * E + ei
                        nx = gi + 2
                        nxt = (nx // E, nx % E)
                        if nxt[0] < NST2 and nxt not in eload:
                            eload[nxt] = (exp_load(
                                nxt[1], nx % 3,
                                first_pass=(nxt[0] == 0 or EXP_NOCACHE),
                                deps=[guard.get(f"eb{nx % 3}")]),
                                nx % 3)
                    if 2 <= ei < 2 + NC2 and st + 1 < NST2:
                        prelude_chunk(st + 1, ei - 2)
                    pend = (ei, pt)
                down(pend[0], st, pend[1])
                out_done = sV.tok
                b = st % 2
                n0 = st * (ST2_ // 128)
                stok = qS.do(lambda s, n0=n0, b=b: s.dma_start(
                    out=out_h.rearrange("(n p) f -> p n f", p=128)[
                        :, n0:n0 + NC2, :],
                    in_=x1t_[:].rearrange("p (n f) -> p n f", n=NC2)),
                    sig=dS, deps=[out_done])
                guard["x1t"] = stok

            flush()

        fin_sems = [dS, dY, dP, dR, dX, dA, dX2, dA2, dCW, dEC, dT,
                    w1s[0], w1s[1], w2s[0], w2s[1],
                    w1y[0], w1y[1], w2y[0], w2y[1],
                    sT, sV, sS] + [c for sl in eSs + eYs for c in sl]
        blk.gpsimd(lambda g: [g.wait_ge(c.h, c.n) for c in fin_sems if c.n])
    return nc


def _stage_inputs(inputs):
    f = np.float32
    d = {}
    d["Wq"] = np.ascontiguousarray(inputs["L0_attn_Wq"], f)
    d["Wk"] = np.ascontiguousarray(inputs["L0_attn_Wk"], f)
    d["Wv"] = np.ascontiguousarray(inputs["L0_attn_Wv"], f)
    d["Wg"] = np.ascontiguousarray(inputs["L0_attn_Wg"], f)
    d["Wo"] = np.ascontiguousarray(inputs["L0_attn_Wo"], f)
    d["W1"] = np.ascontiguousarray(inputs["L0_attn_W1"], f)
    d["W2"] = np.ascontiguousarray(inputs["L0_attn_W2"], f)
    gk1 = np.asarray(inputs["L0_attn_gk_w1"], f)
    d["gk1pad"] = np.concatenate([gk1, np.zeros((D, 16), f)], axis=1)
    gk2 = np.asarray(inputs["L0_attn_gk_w2"], f)
    gkb = np.asarray(inputs["L0_attn_gk_b2"], f).reshape(1, QF)
    d["gk2ext"] = np.concatenate([gk2, gkb, np.zeros((15, QF), f)], axis=0)
    d["ln_w"] = np.asarray(inputs["L0_attn_ln_w"], f).reshape(1, D)
    d["ln_b"] = np.asarray(inputs["L0_attn_ln_b"], f).reshape(1, D)
    d["rms_w"] = np.asarray(inputs["L0_attn_rms_w"], f).reshape(1, DV)
    d["b1"] = np.ascontiguousarray(
        np.asarray(inputs["L0_attn_b1"], f).reshape(16, 128).T)
    d["b2"] = np.asarray(inputs["L0_attn_b2"], f).reshape(1, D)
    d["gate_wT"] = np.ascontiguousarray(np.asarray(inputs["L0_ffn_gate_w"], f).T)
    d["e_wg"] = np.ascontiguousarray(inputs["L0_ffn_w_gate"], f)
    d["e_wu"] = np.ascontiguousarray(inputs["L0_ffn_w_up"], f)
    d["e_wd"] = np.ascontiguousarray(inputs["L0_ffn_w_down"], f)
    d["ws_gate"] = np.ascontiguousarray(inputs["L0_ffn_ws_gate"], f)
    d["ws_up"] = np.ascontiguousarray(inputs["L0_ffn_ws_up"], f)
    d["ws_down"] = np.ascontiguousarray(inputs["L0_ffn_ws_down"], f)
    return d


def kernel(**inputs):
    x = np.asarray(inputs["L0_attn_x"], np.float32)
    B, T, _ = x.shape
    shared = _stage_inputs(inputs)
    nc = build_program(T, taps=DEBUG_TAPS)
    in_maps = []
    for c in range(B):
        m = {"x": np.ascontiguousarray(x[c])}
        m.update(shared)
        in_maps.append(m)
    res = run_bass_kernel_spmd(nc, in_maps, core_ids=list(range(B)))
    out = np.stack([r["out"] for r in res.results], axis=0)
    return out.astype(np.float32)


# version 16
# speedup vs baseline: 1.0457x; 1.0457x over previous
import numpy as np
import ml_dtypes
from contextlib import ExitStack

import concourse.bass as bass
import concourse.mybir as mybir
from concourse.bass_utils import run_bass_kernel_spmd

F32 = mybir.dt.float32
BF16 = mybir.dt.bfloat16
AF = mybir.ActivationFunctionType
ALU = mybir.AluOpType
AX = mybir.AxisListType

D = 1024
HEADS = 16
DK = 32
DV = 64
QF = HEADS * DK
C = 128
ST1 = 512
ST2 = 1024
E = 16
M = 512
EPS = 1e-5
GATE_NORM = 16.0
DKS = DK ** -0.5

DEBUG_TAPS = []
P1L = 99
P1G = 99
EXP_NOCACHE = False


class Ctr:
    def __init__(self, es, nc, name, step=1):
        self.h = es.enter_context(nc.semaphore(name))
        self.step = step
        self.n = 0

    def bump(self, k=1):
        self.n += k * self.step
        return (self, self.n)

    @property
    def tok(self):
        return (self, self.n)


class EQ:
    def __init__(self, name):
        self.name = name
        self.ops = []
        self.waited = {}

    def wait(self, tok):
        if tok is None:
            return
        if isinstance(tok, (list, tuple)) and tok and isinstance(tok[0], (list, tuple)):
            for t in tok:
                self.wait(t)
            return
        ctr, v = tok
        if v <= 0 or self.waited.get(id(ctr), -1) >= v:
            return
        self.waited[id(ctr)] = v
        h = ctr.h
        self.ops.append(lambda e, h=h, v=v: e.wait_ge(h, v))

    def do(self, fn, sig=None, deps=(), k=1):
        for d in deps:
            self.wait(d)
        if sig is not None and sig.step == 16:
            self.wait(sig.tok)
        if sig is not None:
            tok = sig.bump(k)
            h, step = sig.h, sig.step * k
            self.ops.append(lambda e, fn=fn, h=h, s=step: fn(e).then_inc(h, s))
            return tok
        self.ops.append(lambda e, fn=fn: fn(e))
        return None


def _bc(ap, shape):
    return ap.to_broadcast(shape)


def build_program(T_core, taps=(), phases=3):
    taps = set(taps)
    nc = bass.Bass("TRN2", target_bir_lowering=False)
    NST1 = T_core // ST1
    ST2_ = ST2 if T_core >= ST2 else T_core
    NST2 = (T_core // ST2_) if (phases & 2) else 0
    NC1 = ST1 // C
    NC2 = ST2_ // C

    xin_h = nc.dram_tensor("x", [T_core, D], F32, kind="ExternalInput").ap()
    out_h = nc.dram_tensor("out", [T_core, D], F32, kind="ExternalOutput").ap()
    wq_h = nc.dram_tensor("Wq", [D, QF], F32, kind="ExternalInput").ap()
    wk_h = nc.dram_tensor("Wk", [D, QF], F32, kind="ExternalInput").ap()
    wv_h = nc.dram_tensor("Wv", [D, D], F32, kind="ExternalInput").ap()
    wg_h = nc.dram_tensor("Wg", [D, D], F32, kind="ExternalInput").ap()
    wo_h = nc.dram_tensor("Wo", [D, D], F32, kind="ExternalInput").ap()
    w1_h = nc.dram_tensor("W1", [D, 2 * D], F32, kind="ExternalInput").ap()
    w2_h = nc.dram_tensor("W2", [2 * D, D], F32, kind="ExternalInput").ap()
    gk1_h = nc.dram_tensor("gk1pad", [D, 32], F32, kind="ExternalInput").ap()
    gk2_h = nc.dram_tensor("gk2ext", [32, QF], F32, kind="ExternalInput").ap()
    lnw_h = nc.dram_tensor("ln_w", [1, D], F32, kind="ExternalInput").ap()
    lnb_h = nc.dram_tensor("ln_b", [1, D], F32, kind="ExternalInput").ap()
    rmsw_h = nc.dram_tensor("rms_w", [1, DV], F32, kind="ExternalInput").ap()
    b1_h = nc.dram_tensor("b1", [128, 16], F32, kind="ExternalInput").ap()
    b2_h = nc.dram_tensor("b2", [1, D], F32, kind="ExternalInput").ap()
    gwt_h = nc.dram_tensor("gate_wT", [D, E], F32, kind="ExternalInput").ap()
    ewg_h = nc.dram_tensor("e_wg", [E, D, M], F32, kind="ExternalInput").ap()
    ewu_h = nc.dram_tensor("e_wu", [E, D, M], F32, kind="ExternalInput").ap()
    ewd_h = nc.dram_tensor("e_wd", [E, M, D], F32, kind="ExternalInput").ap()
    swg_h = nc.dram_tensor("ws_gate", [D, M], F32, kind="ExternalInput").ap()
    swu_h = nc.dram_tensor("ws_up", [D, M], F32, kind="ExternalInput").ap()
    swd_h = nc.dram_tensor("ws_down", [M, D], F32, kind="ExternalInput").ap()
    x1scr = nc.dram_tensor("x1scr", [T_core, D], F32, kind="Internal").ap()
    w1c_h = nc.dram_tensor("w1cache", [4, 128, 8 * 512], BF16,
                           kind="Internal").ap()
    w2c_h = nc.dram_tensor("w2cache", [4, 128, 4 * 1024], BF16,
                           kind="Internal").ap()
    ewgc_h = nc.dram_tensor("ewgc", [E, 128, 8 * M], BF16, kind="Internal").ap()
    ewuc_h = nc.dram_tensor("ewuc", [E, 128, 8 * M], BF16, kind="Internal").ap()
    ewdc_h = nc.dram_tensor("ewdc", [E, 128, 4 * D], BF16, kind="Internal").ap()
    tap_h = {}
    for t in taps:
        tap_h[t] = nc.dram_tensor("tap_" + t, [T_core, 2 * D], F32,
                                  kind="ExternalOutput").ap()

    iden_b = nc.inline_tensor(np.eye(C, dtype=ml_dtypes.bfloat16), name="idenb").ap()
    iden_f = nc.inline_tensor(np.eye(C, dtype=np.float32), name="idenf").ap()
    um = np.zeros((C, C), np.float32)
    for s_ in range(C):
        um[s_, s_:] = -1.0 / GATE_NORM
    um_c = nc.inline_tensor(um, name="umtri").ap()
    mAT = np.zeros((C, C), np.float32)
    for s_ in range(C):
        mAT[s_, s_:] = 1.0
    mAT_c = nc.inline_tensor(mAT, name="maskAT").ap()
    onescol_c = nc.inline_tensor(np.ones((1, C), np.float32), name="onescol").ap()
    onescolb_c = nc.inline_tensor(
        np.ones((1, C), ml_dtypes.bfloat16), name="onescolb").ap()
    negsix_c = nc.inline_tensor(
        np.full((C, 1), -1.0 / GATE_NORM, np.float32), name="negsix").ap()
    e16s = np.zeros((1, E * E), np.float32)
    for h in range(E):
        e16s[0, E * h + h] = 1.0
    e16s_c = nc.inline_tensor(e16s, name="e16s").ap()
    e16col = np.zeros((1, 32), ml_dtypes.bfloat16)
    e16col[0, 16] = 1.0
    e16col_c = nc.inline_tensor(e16col, name="e16col").ap()
    onesrow_b = nc.inline_tensor(
        np.ones((1, ST1), ml_dtypes.bfloat16), name="onesrowb").ap()

    es = ExitStack()
    with es:
        sT = Ctr(es, nc, "sT")
        sV = Ctr(es, nc, "sV")
        sS = Ctr(es, nc, "sS")
        sP = Ctr(es, nc, "sP")
        dY = Ctr(es, nc, "dY", 16)
        dS = Ctr(es, nc, "dS", 16)
        dP = Ctr(es, nc, "dP", 16)
        dR = Ctr(es, nc, "dR", 16)
        dX = Ctr(es, nc, "dX", 16)
        dA = Ctr(es, nc, "dA", 16)
        dCW = Ctr(es, nc, "dCW", 16)
        dEC = Ctr(es, nc, "dEC", 16)
        dT = Ctr(es, nc, "dT", 16)
        dX2 = Ctr(es, nc, "dX2", 16)
        dA2 = Ctr(es, nc, "dA2", 16)
        w1s = [Ctr(es, nc, f"w1s{i}", 16) for i in range(2)]
        w2s = [Ctr(es, nc, f"w2s{i}", 16) for i in range(2)]
        eSs = [[Ctr(es, nc, f"eS{i}_{j}", 16) for j in range(3)]
               for i in range(3)]
        w1y = [Ctr(es, nc, f"w1y{i}", 16) for i in range(2)]
        w2y = [Ctr(es, nc, f"w2y{i}", 16) for i in range(2)]
        eYs = [[Ctr(es, nc, f"eY{i}_{j}", 16) for j in range(3)]
               for i in range(3)]
        qT, qV, qS, qP, qY = EQ("T"), EQ("V"), EQ("S"), EQ("P"), EQ("Y")
        pf_tok = {}
        pre_ln1 = {}

        pAT = es.enter_context(nc.psum_tensor("pAT", [128, 1024], F32))
        pO = es.enter_context(nc.psum_tensor("pO", [128, 1024], F32))
        pP0 = es.enter_context(nc.psum_tensor("pP0", [128, 512], F32))
        pP1 = es.enter_context(nc.psum_tensor("pP1", [128, 512], F32))
        pT = es.enter_context(nc.psum_tensor("pT", [128, 1024], BF16))
        pQx = es.enter_context(nc.psum_tensor("pQx", [128, 512], F32))
        pP = [pP0, pP1]
        guard = {}

        def bw(q, name):
            q.wait(guard.get(name))

        wtile = es.enter_context(nc.sbuf_tensor("wtile", [128, D], BF16))
        btile = es.enter_context(nc.sbuf_tensor("btile", [128, D], BF16))
        rmstile = es.enter_context(nc.sbuf_tensor("rmstile", [128, D], BF16))
        b1col = es.enter_context(nc.sbuf_tensor("b1col", [128, 16], F32))
        b2row = es.enter_context(nc.sbuf_tensor("b2row", [1, D], BF16))
        onesb_s = es.enter_context(nc.sbuf_tensor("onesb_s", [1, C], BF16))
        cst1 = es.enter_context(nc.sbuf_tensor("cst1", [128, 1], F32))
        iden_s = es.enter_context(nc.sbuf_tensor("iden_s", [128, C], BF16))
        idenf_s = es.enter_context(nc.sbuf_tensor("idenf_s", [128, C], F32))
        um_s = es.enter_context(nc.sbuf_tensor("um_s", [128, C], F32))
        mAT_s = es.enter_context(nc.sbuf_tensor("mAT_s", [128, C], F32))
        ones_s = es.enter_context(nc.sbuf_tensor("ones_s", [1, C], F32))
        nsix_s = es.enter_context(nc.sbuf_tensor("nsix_s", [128, 1], F32))
        e16s_s = es.enter_context(nc.sbuf_tensor("e16s_s", [1, E * E], F32))
        e16c_s = es.enter_context(nc.sbuf_tensor("e16c_s", [1, 32], BF16))
        onesr_s = es.enter_context(nc.sbuf_tensor("onesr_s", [1, ST1], BF16))
        es0 = ExitStack()
        lnw_r = es0.enter_context(nc.sbuf_tensor("lnw_r", [1, D], F32))
        lnb_r = es0.enter_context(nc.sbuf_tensor("lnb_r", [1, D], F32))
        rms_r = es0.enter_context(nc.sbuf_tensor("rms_r", [1, DV], F32))

        for dst, src in [
            (iden_s[:], iden_b[:]), (idenf_s[:], iden_f[:]),
            (um_s[:], um_c[:]), (mAT_s[:], mAT_c[:]),
            (ones_s[:], onescol_c[:]),
            (nsix_s[:], negsix_c[:]), (e16s_s[:], e16s_c[:]),
            (onesb_s[:], onescolb_c[:]),
            (e16c_s[:], e16col_c[:]), (onesr_s[:], onesrow_b[:]),
            (lnw_r[:], lnw_h[:]), (lnb_r[:], lnb_h[:]),
            (rms_r[:], rmsw_h[:]),
            (b1col[:], b1_h[:]),
        ]:
            qY.do(lambda y, d=dst, s=src: y.dma_start(out=d, in_=s), sig=dY)
        c_loaded = dY.tok

        qV.do(lambda v: v.memset(cst1[:], 1.0), sig=sV)
        qP.do(lambda g: g.dma_start(out=b2row[:], in_=b2_h[:]), sig=dR)

        def bcast_row(row, width, dst):
            w = min(512, width)
            nh = max(1, width // 512)
            for half in range(nh):
                bw(qT, "pQx")
                t0 = qT.do(lambda t, h=half, w=w, r=row: t.matmul(
                    pQx[:, 0:w], ones_s[0:1, :], r[0:1, h * 512:h * 512 + w],
                    start=True, stop=True), sig=sT, deps=[c_loaded])
                tok = qV.do(lambda v, h=half, w=w: v.tensor_copy(
                    dst[:, h * 512:h * 512 + w], pQx[:, 0:w]), sig=sV,
                    deps=[t0])
                guard["pQx"] = tok

        bcast_row(lnw_r, D, wtile)
        bcast_row(lnb_r, D, btile)
        bcast_row(rms_r, DV, rmstile)
        qV.do(lambda v: v.tensor_copy(
            rmstile[:].rearrange("p (h v) -> p h v", h=HEADS)[:, 1:, :],
            _bc(rmstile[:, None, 0:DV], [128, HEADS - 1, DV])), sig=sV,
            deps=[sV.tok])
        consts_ready = [sV.tok, sT.tok, c_loaded]
        es0.close()

        blk = es.enter_context(nc.Block())

        def flush():
            if qT.ops:
                ops = qT.ops; qT.ops = []
                blk.tensor(lambda t, ops=ops: [op(t) for op in ops])
            if qV.ops:
                ops = qV.ops; qV.ops = []
                blk.vector(lambda v, ops=ops: [op(v) for op in ops])
            if qS.ops:
                ops = qS.ops; qS.ops = []
                blk.scalar(lambda s, ops=ops: [op(s) for op in ops])
            if qP.ops:
                ops = qP.ops; qP.ops = []
                blk.gpsimd(lambda g, ops=ops: [op(g) for op in ops])
            if qY.ops:
                ops = qY.ops; qY.ops = []
                blk.sync(lambda y, ops=ops: [op(y) for op in ops])

        es1 = ExitStack()
        with es1:
            wq_s = es1.enter_context(nc.sbuf_tensor("wq_s", [128, 8 * 512], BF16))
            wk_s = es1.enter_context(nc.sbuf_tensor("wk_s", [128, 8 * 512], BF16))
            wv_s = es1.enter_context(nc.sbuf_tensor("wv_s", [128, 8 * 1024], BF16))
            wgg_s = es1.enter_context(nc.sbuf_tensor("wgg_s", [128, 8 * 1024], BF16))
            wo_s = es1.enter_context(nc.sbuf_tensor("wo_s", [128, 8 * 1024], BF16))
            gk1_s = es1.enter_context(nc.sbuf_tensor("gk1_s", [128, 8 * 32], BF16))
            gk2_s = es1.enter_context(nc.sbuf_tensor("gk2_s", [32, QF], BF16))
            w1b = [es1.enter_context(
                nc.sbuf_tensor(f"w1b{i}", [128, 8 * 512], BF16)) for i in range(2)]
            w2b = [es1.enter_context(
                nc.sbuf_tensor(f"w2b{i}", [128, 4 * 1024], BF16)) for i in range(2)]

            def cast_load(dst_view, src_view, ctr, deps=()):
                return qP.do(lambda g, d=dst_view, s=src_view: g.dma_start(
                    out=d, in_=s), sig=ctr, deps=deps)

            cast_load(wq_s[:].rearrange("p (k n) -> p k n", k=8),
                      wq_h.rearrange("(k p) n -> p k n", p=128), dR,
                      deps=consts_ready)
            cast_load(wk_s[:].rearrange("p (k n) -> p k n", k=8),
                      wk_h.rearrange("(k p) n -> p k n", p=128), dR)
            cast_load(wv_s[:].rearrange("p (k n) -> p k n", k=8),
                      wv_h.rearrange("(k p) n -> p k n", p=128), dR)
            cast_load(wgg_s[:].rearrange("p (k n) -> p k n", k=8),
                      wg_h.rearrange("(k p) n -> p k n", p=128), dR)
            cast_load(wo_s[:].rearrange("p (k n) -> p k n", k=8),
                      wo_h.rearrange("(k p) n -> p k n", p=128), dR)
            cast_load(gk1_s[:].rearrange("p (k q) -> p k q", k=8),
                      gk1_h.rearrange("(k p) q -> p k q", p=128), dR)
            cast_load(gk2_s[:], gk2_h[:], dR)
            attnw_ready = dR.tok

            w1pf = []
            w2pf = []
            if phases & 1:
                pfsem = [eSs[0][0], eSs[1][0], eSs[2][0], eSs[0][1],
                         eSs[1][1], eSs[2][1], eSs[0][2], eSs[1][2]]
                for qu in range(4):
                    w1pf.append(qP.do(lambda g, qu=qu: g.dma_start(
                        out=w1c_h[qu].rearrange("p (k n) -> p k n", k=8),
                        in_=w1_h.rearrange("(k p) n -> p k n", p=128)[
                            :, :, qu * 512:(qu + 1) * 512]),
                        sig=pfsem[qu],
                        deps=consts_ready if qu == 0 else ()))
                for qu in range(4):
                    w2pf.append(qP.do(lambda g, qu=qu: g.dma_start(
                        out=w2c_h[qu].rearrange("p (u n) -> p u n", u=4),
                        in_=w2_h.rearrange("(u p) n -> p u n", p=128)[
                            :, qu * 4:(qu + 1) * 4, :]),
                        sig=pfsem[4 + qu]))

            xc_buf = [es1.enter_context(
                nc.sbuf_tensor(f"xc_buf{i}", [128, D], F32)) for i in range(2)]
            xs = es1.enter_context(nc.sbuf_tensor("xs", [128, NC1 * D], F32))
            hT = es1.enter_context(nc.sbuf_tensor("hT", [128, 8 * ST1], BF16))
            htok = es1.enter_context(nc.sbuf_tensor("htok", [128, D], BF16))
            qTf = es1.enter_context(nc.sbuf_tensor("qTf", [128, NC1 * 512], BF16))
            kTf = es1.enter_context(nc.sbuf_tensor("kTf", [128, NC1 * 512], BF16))
            qgT_hf = es1.enter_context(nc.sbuf_tensor("qgT_hf", [32, 8 * C], BF16))
            kgT_hf = es1.enter_context(nc.sbuf_tensor("kgT_hf", [32, 8 * C], BF16))
            vTok = [es1.enter_context(
                nc.sbuf_tensor(f"vTok{i}", [128, D], BF16)) for i in range(2)]
            gs = [es1.enter_context(
                nc.sbuf_tensor(f"gs{i}", [128, D], BF16)) for i in range(2)]
            gktok = es1.enter_context(nc.sbuf_tensor("gktok", [128, NC1 * QF], F32))
            eq_c = es1.enter_context(nc.sbuf_tensor("eq_c", [128, 4 * C], F32))
            ek_c = es1.enter_context(nc.sbuf_tensor("ek_c", [128, 4 * C], F32))
            kdt_c = [es1.enter_context(
                nc.sbuf_tensor(f"kdt_c{i}", [128, 512], BF16))
                for i in range(2)]
            atm_c = es1.enter_context(nc.sbuf_tensor("atm_c", [128, HEADS * C], BF16))
            ogT_c = es1.enter_context(nc.sbuf_tensor("ogT_c", [128, 8 * C], BF16))
            gelu_q = [es1.enter_context(
                nc.sbuf_tensor(f"gelu_q{i}", [128, 4 * ST1], BF16))
                for i in range(2)]
            t1ext = es1.enter_context(nc.sbuf_tensor("t1ext", [32, ST1], BF16))
            S_sb = es1.enter_context(nc.sbuf_tensor("S_sb", [32, HEADS * DV], F32))
            S_bf = es1.enter_context(nc.sbuf_tensor("S_bf", [32, HEADS * DV], BF16))
            dec_sb = [es1.enter_context(
                nc.sbuf_tensor(f"dec_sb{i}", [32, HEADS], F32))
                for i in range(2)]
            st_neg = es1.enter_context(nc.sbuf_tensor("st_neg", [128, 8], F32))
            st_ssq = es1.enter_context(nc.sbuf_tensor("st_ssq", [128, 8], F32))
            st_a = es1.enter_context(nc.sbuf_tensor("st_a", [128, 8], F32))
            st_b = es1.enter_context(nc.sbuf_tensor("st_b", [128, 8], F32))
            st_c = es1.enter_context(nc.sbuf_tensor("st_c", [128, 8], F32))
            st_d = es1.enter_context(nc.sbuf_tensor("st_d", [128, 8], F32))
            ss16 = es1.enter_context(nc.sbuf_tensor("ss16", [128, HEADS], F32))
            ri16 = es1.enter_context(nc.sbuf_tensor("ri16", [128, HEADS], F32))
            rg16 = es1.enter_context(nc.sbuf_tensor("rg16", [128, HEADS], F32))
            tscr = es1.enter_context(nc.sbuf_tensor("tscr", [128, 512], F32))
            tscr2 = es1.enter_context(nc.sbuf_tensor("tscr2", [128, 512], F32))
            esb = tscr

            qV.do(lambda v: v.memset(S_sb[:], 0.0), sig=sV)
            qV.do(lambda v: v.memset(S_bf[:], 0.0), sig=sV)

            x1_stored = []

            def tap(name, buf_view, rows, idx, toks, width):
                if name not in taps:
                    return
                qP.do(lambda g, b=buf_view, w=width, i=idx, r=rows:
                      g.dma_start(out=tap_h[name][i * r:(i + 1) * r, 0:w],
                                  in_=b), sig=dT, deps=toks)

            I32 = mybir.dt.int32
            MAGIC = 0x5f3759df

            def rsqrt_v(q, dst, src, scr, deps):
                t = q.do(lambda e, d=dst, s=src: e.tensor_scalar(
                    d.bitcast(I32), s.bitcast(I32), 1, None,
                    ALU.logical_shift_right), sig=sV if q is qV else sP,
                    deps=deps)
                sig = sV if q is qV else sP
                t = q.do(lambda e, d=dst: e.tensor_scalar(
                    d.bitcast(I32), d.bitcast(I32), MAGIC, -1,
                    ALU.subtract, ALU.mult), sig=sig, deps=[t])
                for _ in range(3):
                    t = q.do(lambda e, d=dst, sc=scr: e.tensor_mul(
                        sc, d, d), sig=sig, deps=[t])
                    t = q.do(lambda e, s=src, sc=scr: e.tensor_mul(
                        sc, sc, s), sig=sig, deps=[t])
                    t = q.do(lambda e, sc=scr: e.tensor_scalar(
                        sc, sc, -0.5, 1.5, ALU.mult, ALU.add), sig=sig,
                        deps=[t])
                    t = q.do(lambda e, d=dst, sc=scr: e.tensor_mul(
                        d, d, sc), sig=sig, deps=[t])
                return t

            def ln_stats(src_view, two_ln, ci):
                cw = lambda t_: t_[:, ci:ci + 1]
                r = qV.do(lambda v: v.reduce_sum(
                    cw(st_neg), src_view, axis=AX.X, negate=True), sig=sV,
                    deps=[guard.get(f"lnst{ci}")])
                bw(qS, "pQx")
                a = qS.do(lambda s, sv=src_view: s.activation(
                    pQx[:, 0:512], sv[:, 0:512], AF.Square,
                    accum_out=cw(st_ssq)), sig=sS,
                    deps=[guard.get(f"lnsa{ci}")])
                a = qS.do(lambda s, sv=src_view: s.activation(
                    pQx[:, 0:512], sv[:, 512:1024], AF.Square,
                    accum_out=cw(st_a)), sig=sS, deps=[a])
                guard["pQx"] = a
                r = qV.do(lambda v: v.tensor_scalar_mul(
                    cw(st_neg), cw(st_neg), 1.0 / D), sig=sV, deps=[r])
                v1 = qV.do(lambda v: v.tensor_add(
                    cw(st_ssq), cw(st_ssq), cw(st_a)), sig=sV, deps=[a, r])
                v1 = qV.do(lambda v: v.tensor_scalar_mul(
                    cw(st_a), cw(st_ssq), 1.0 / D), sig=sV, deps=[v1])
                v1 = qV.do(lambda v: v.tensor_mul(
                    cw(st_b), cw(st_neg), cw(st_neg)), sig=sV, deps=[v1])
                v1 = qV.do(lambda v: v.tensor_sub(
                    cw(st_a), cw(st_a), cw(st_b)), sig=sV, deps=[v1])
                v1 = qV.do(lambda v: v.tensor_scalar_add(
                    cw(st_b), cw(st_a), EPS), sig=sV, deps=[v1])
                if two_ln:
                    v1 = qV.do(lambda v: v.reciprocal(
                        cw(st_c), cw(st_b)), sig=sV, deps=[v1])
                    v1 = qV.do(lambda v: v.tensor_mul(
                        cw(st_c), cw(st_c), cw(st_a)), sig=sV, deps=[v1])
                    v1 = qV.do(lambda v: v.tensor_scalar_add(
                        cw(st_c), cw(st_c), EPS), sig=sV, deps=[v1])
                    v1 = qV.do(lambda v: v.tensor_mul(
                        cw(st_c), cw(st_c), cw(st_b)), sig=sV, deps=[v1])
                    src = st_c
                else:
                    src = st_b
                a2 = qS.do(lambda s, sr=src: s.activation(
                    cw(st_d), cw(sr), AF.Sqrt), sig=sS, deps=[v1])
                v1 = qV.do(lambda v: v.reciprocal(
                    cw(st_c), cw(st_d)), sig=sV, deps=[a2])
                last_act = a2
                if two_ln:
                    a3 = qS.do(lambda s: s.activation(
                        cw(st_d), cw(st_b), AF.Sqrt), sig=sS, deps=[v1])
                    v1 = qV.do(lambda v: v.reciprocal(
                        cw(st_b), cw(st_d)), sig=sV, deps=[a3])
                    last_act = a3
                guard[f"lnsa{ci}"] = last_act
                guard[f"lnst{ci}"] = v1
                return v1

            def transpose_to(dst_view_fn, src_view_fn, nblk, deps, evac_deps=()):
                bw(qT, "pT")
                t0 = None
                for b in range(nblk):
                    t0 = qT.do(lambda t, b=b: t.transpose(
                        pT[:, b * C:(b + 1) * C], src_view_fn(b), iden_s[:]),
                        sig=sT if b == nblk - 1 else None,
                        deps=deps if b == 0 else ())
                ev = qV.do(lambda v: v.tensor_copy(
                    dst_view_fn(), pT[:, 0:nblk * C]), sig=sV,
                    deps=[t0] + list(evac_deps))
                guard["pT"] = ev
                return ev

            def ln_block(x_view_fn, two_ln, nchunks, hT_dst, tapname,
                         tapbase, loader=None, src_tok=None, pre=None):
                stoks = dict(pre) if pre else {}
                xviews = {}

                def do_stats(c):
                    if c not in stoks:
                        if loader:
                            lt = loader(c)
                            qV.wait(lt)
                            qS.wait(lt)
                        if src_tok is not None:
                            qV.wait(src_tok)
                            qS.wait(src_tok)
                        stoks[c] = ln_stats(x_view_fn(c), two_ln, c)
                    xviews[c] = x_view_fn(c)

                nup = nchunks if loader is None else min(2, nchunks)
                for c in range(nup):
                    do_stats(c)
                for c in range(nchunks):
                    xc = xviews[c]
                    stok = stoks[c]
                    cwn = st_neg[:, c:c + 1]
                    cwc = st_c[:, c:c + 1]
                    qh, sh = (qP, sP) if c % 2 == 0 else (qV, sV)
                    tb, tg = (tscr, "tscr") if c % 2 == 0 else (tscr2, "tscr2")
                    h1 = None
                    for hh in range(2):
                        h1 = qV.do(lambda v, xc=xc, hh=hh, cwn=cwn, cwc=cwc,
                                   tb=tb: v.tensor_scalar(
                                       tb[:], xc[:, hh * 512:hh * 512 + 512],
                                       cwn, cwc, ALU.add, ALU.mult), sig=sV,
                                   deps=([stok] if hh == 0 else [h1])
                                   + [guard.get(tg)])
                        h1 = qh.do(lambda g, hh=hh, tb=tb: g.tensor_mul(
                            tb[:], tb[:], wtile[:, hh * 512:hh * 512 + 512]),
                            sig=sh, deps=[h1])
                        h1 = qh.do(lambda g, hh=hh, tb=tb: g.tensor_add(
                            htok[:, hh * 512:hh * 512 + 512], tb[:],
                            btile[:, hh * 512:hh * 512 + 512]), sig=sh,
                            deps=[h1, guard.get("htok")])
                        guard[tg] = h1
                    ev = transpose_to(
                        lambda c=c: hT_dst[:].rearrange(
                            "p (k t) -> p k t", k=8)[:, :, c * C:(c + 1) * C],
                        lambda b: htok[:, b * C:(b + 1) * C], 8, [h1])
                    guard["htok"] = sT.tok
                    if two_ln:
                        wtok = qV.do(lambda v, xc=xc, c=c, cwn=cwn:
                                     v.tensor_scalar(
                                         xs[:, c * D:(c + 1) * D], xc, cwn,
                                         st_b[:, c:c + 1], ALU.add, ALU.mult),
                                     sig=sV,
                                     deps=[stok,
                                           guard.get(f"xs_store{c}")])
                        guard[f"xc{c % 2}"] = wtok
                        guard[f"xcv{c % 2}"] = wtok
                    if nup <= c + 1 < nchunks:
                        do_stats(c + 1)
                    tap(tapname, htok[:], 128, tapbase + c, [h1], D)

            for st in range(NST1 if (phases & 1) else 0):
                n0 = st * NC1
                def xload(c, n0=n0):
                    return qS.do(lambda s, c=c, n0=n0: s.dma_start(
                        out=xc_buf[c % 2][:],
                        in_=xin_h.rearrange("(n p) f -> p n f", p=128)[
                            :, n0 + c, :]),
                        sig=(dX if c % 2 == 0 else dA),
                        deps=consts_ready + [guard.get(f"xc{c % 2}"),
                                             guard.get(f"xcv{c % 2}")])

                if P1L >= 1:
                    ln_block(lambda c: xc_buf[c % 2][:], True, NC1,
                             hT, "h", st * NC1, loader=xload,
                             pre=pre_ln1.pop(st, None))
                    hT_all = [sV.tok]

                if P1L >= 2:
                    bw(qT, "pQx")
                    mm = None
                    for k in range(8):
                        mm = qT.do(lambda t, k=k: t.matmul(
                            pQx[0:32, :], gk1_s[:].rearrange(
                                "p (k q) -> p k q", k=8)[:, k, :],
                            hT[:, k * ST1:(k + 1) * ST1], start=(k == 0),
                            stop=False),
                            deps=(hT_all + [attnw_ready]) if k == 0 else ())
                    mm = qT.do(lambda t: t.matmul(
                        pQx[0:32, :], e16c_s[:], onesr_s[:], start=False,
                        stop=True), sig=sT)
                    ev1 = qV.do(lambda v: v.tensor_copy(t1ext[:], pQx[0:32, :]),
                                sig=sV, deps=[mm])
                    guard["pQx"] = ev1
                    for c in range(NC1):
                        pb = pP[c % 2]
                        bw(qT, f"pP{c % 2}")
                        mm = qT.do(lambda t, c=c, pb=pb: t.matmul(
                            pb[:, 0:QF], t1ext[:, c * C:(c + 1) * C], gk2_s[:],
                            start=True, stop=True), sig=sT, deps=[ev1])
                        a1 = qS.do(lambda s, pb=pb: s.activation(
                            esb[:, 0:QF], pb[:, 0:QF], AF.Exp, scale=-1.0), sig=sS,
                            deps=[mm, guard.get("esb")])
                        guard[f"pP{c % 2}"] = a1
                        a1 = qS.do(lambda s, c=c: s.activation(
                            gktok[:, c * QF:(c + 1) * QF], esb[:, 0:QF], AF.Ln,
                            bias=cst1[:, 0:1]), sig=sS, deps=[a1])
                        guard["esb"] = a1
                        tap("gk", gktok[:, c * QF:(c + 1) * QF], 128,
                            st * NC1 + c, [a1], QF)
                    gk_done = sS.tok

                if P1L >= 3:
                    for (wbuf, dst) in ((wq_s, qTf), (wk_s, kTf)):
                        for c in range(NC1):
                            pb = pP[c % 2]
                            bw(qT, f"pP{c % 2}")
                            mm = None
                            for k in range(8):
                                mm = qT.do(
                                    lambda t, k=k, c=c, wbuf=wbuf, pb=pb:
                                    t.matmul(
                                        pb[:, 0:512],
                                        hT[:].rearrange(
                                            "p (k t) -> p k t", k=8)[
                                            :, k, c * C:(c + 1) * C],
                                        wbuf[:].rearrange(
                                            "p (k n) -> p k n", k=8)[:, k, :],
                                        start=(k == 0), stop=(k == 7)),
                                    sig=sT if k == 7 else None,
                                    deps=hT_all if k == 0 else ())
                            ev = qV.do(lambda v, c=c, dst=dst, pb=pb:
                                       v.tensor_copy(
                                           dst[:, c * 512:(c + 1) * 512],
                                           pb[:, 0:512]), sig=sV,
                                       deps=[mm, guard.get("qk_w")])
                            guard[f"pP{c % 2}"] = ev
                    qk_done = sV.tok
                    tap("qT", qTf[:, 0:512], 128, st, [qk_done], 512)

                if P1L >= 5:
                    vgtok = {}

                    def vg_proj(c):
                        par = c % 2
                        toks = []
                        for (wbuf, is_g) in ((wv_s, False), (wgg_s, True)):
                            for hf in range(2):
                                pb = pP[hf]
                                bw(qT, f"pP{hf}")
                                mm = None
                                for k in range(8):
                                    mm = qT.do(
                                        lambda t, k=k, c=c, hf=hf, wbuf=wbuf,
                                        pb=pb: t.matmul(
                                            pb[:, 0:512],
                                            hT[:].rearrange(
                                                "p (k t) -> p k t", k=8)[
                                                :, k, c * C:(c + 1) * C],
                                            wbuf[:].rearrange(
                                                "p (k n) -> p k n", k=8)[
                                                :, k,
                                                hf * 512:hf * 512 + 512],
                                            start=(k == 0), stop=(k == 7)),
                                        sig=sT if k == 7 else None,
                                        deps=hT_all if k == 0 else ())
                                col = hf * 512
                                if is_g:
                                    evg = qS.do(
                                        lambda s, col=col, pb=pb, par=par:
                                        s.activation(
                                            gs[par][:, col:col + 512],
                                            pb[:, 0:512], AF.Silu), sig=sS,
                                        deps=[mm, guard.get(f"gs{par}")])
                                    evg = qP.do(
                                        lambda g, col=col, hf=hf, par=par:
                                        g.tensor_mul(
                                            gs[par][:, col:col + 512],
                                            gs[par][:, col:col + 512],
                                            rmstile[:, hf * 512:
                                                    hf * 512 + 512]),
                                        sig=sP, deps=[evg])
                                    guard[f"pP{hf}"] = evg
                                    toks.append(evg)
                                else:
                                    ev = qV.do(
                                        lambda v, col=col, pb=pb, par=par:
                                        v.tensor_copy(
                                            vTok[par][:, col:col + 512],
                                            pb[:, 0:512]), sig=sV,
                                        deps=[mm])
                                    guard[f"pP{hf}"] = ev
                                    toks.append(ev)
                        vgtok[c] = toks

                    def gla_prep(c):
                        par = c % 2
                        bw(qT, "pAT")
                        mm = qT.do(lambda t, c=c: t.matmul(
                            pAT[:, 0:512], um_s[:],
                            gktok[:, c * QF:(c + 1) * QF],
                            start=True, stop=True), sig=sT,
                            deps=[gk_done])
                        a1 = qS.do(lambda s: s.activation(
                            eq_c[:], pAT[:, 0:512], AF.Exp), sig=sS,
                            deps=[mm, guard.get("eqek")])
                        a1 = qS.do(lambda s: s.activation(
                            ek_c[:], pAT[:, 0:512], AF.Exp, scale=-1.0),
                            sig=sS, deps=[a1])
                        exp_done = a1
                        guard["pAT"] = exp_done
                        bw(qT, "pQx")
                        mm = qT.do(lambda t, c=c: t.matmul(
                            pQx[0:1, 0:QF], nsix_s[:],
                            gktok[:, c * QF:(c + 1) * QF],
                            start=True, stop=True), sig=sT,
                            deps=[gk_done])
                        a2 = qS.do(lambda s, c=c: s.activation(
                            gktok[0:1, c * QF:(c + 1) * QF],
                            pQx[0:1, 0:QF], AF.Exp), sig=sS, deps=[mm])
                        mm = None
                        for h in range(HEADS):
                            mm = qT.do(lambda t, h=h, c=c: t.matmul(
                                pQx[0:32, 0:E],
                                gktok[0:1, c * QF + 32 * h:
                                      c * QF + 32 * (h + 1)],
                                e16s_s[0:1, E * h:E * (h + 1)],
                                start=(h == 0), stop=(h == HEADS - 1)),
                                sig=sT if h == HEADS - 1 else None,
                                deps=[a2] if h == 0 else ())
                        dv1 = qV.do(lambda v, par=par: v.tensor_copy(
                            dec_sb[par][:], pQx[0:32, 0:E]), sig=sV,
                            deps=[mm, guard.get(f"dec_p{par}")])
                        mm = qT.do(lambda t, c=c: t.matmul(
                            pQx[:, 0:QF], ones_s[0:1, :],
                            gktok[0:1, c * QF:(c + 1) * QF],
                            start=True, stop=True), sig=sT, deps=[dv1])
                        eklb = mm
                        v3 = qV.do(lambda v, c=c: v.scalar_tensor_tensor(
                            qTf[:, c * 512:(c + 1) * 512],
                            qTf[:, c * 512:(c + 1) * 512],
                            DKS, eq_c[:], ALU.mult, ALU.mult), sig=sV,
                            deps=[exp_done, qk_done])
                        v3 = qV.do(lambda v, c=c: v.tensor_tensor(
                            kTf[:, c * 512:(c + 1) * 512],
                            kTf[:, c * 512:(c + 1) * 512],
                            ek_c[:], ALU.mult), sig=sV, deps=[v3])
                        v3 = qV.do(lambda v, c=c, par=par: v.tensor_tensor(
                            kdt_c[par][:], kTf[:, c * 512:(c + 1) * 512],
                            pQx[:, 0:QF], ALU.mult), sig=sV,
                            deps=[v3, eklb])
                        guard["pQx"] = v3
                        guard["eqek"] = v3
                        return v3, dv1

                    vg_proj(0)
                    prep = gla_prep(0)
                    for c in range(NC1):
                        glob_c = st * NC1 + c
                        par = c % 2
                        qkg_done, dv1 = prep
                        bw(qT, "pO")
                        o_first = []
                        for w in range(2):
                            for (src, dsth) in ((qTf, qgT_hf),
                                                (kTf, kgT_hf)):
                                bw(qT, "pT")
                                t0 = None
                                for i in range(8):
                                    h = w * 8 + i
                                    t0 = qT.do(
                                        lambda t, i=i, h=h, c=c, src=src:
                                        t.transpose(
                                            pT[0:32, i * C:(i + 1) * C],
                                            src[:, c * 512 + 32 * h:
                                                c * 512 + 32 * (h + 1)],
                                            iden_s[:]),
                                        sig=sT if i == 7 else None,
                                        deps=[qkg_done] if i == 0 else ())
                                ev = qV.do(lambda v, dsth=dsth:
                                           v.tensor_copy(
                                               dsth[:],
                                               pT[0:32, 0:8 * C]),
                                           sig=sV, deps=[t0])
                                guard["pT"] = ev
                            tr_ev = ev
                            bw(qT, "pAT")
                            mm = None
                            for i in range(8):
                                mm = qT.do(lambda t, i=i: t.matmul(
                                    pAT[:, i * C:(i + 1) * C],
                                    kgT_hf[0:32, i * C:(i + 1) * C],
                                    qgT_hf[0:32, i * C:(i + 1) * C],
                                    start=True, stop=True),
                                    sig=sT if i == 7 else None,
                                    deps=[tr_ev] if i == 0 else ())
                            atm_ev = qV.do(lambda v, w=w: v.tensor_tensor(
                                atm_c[:].rearrange(
                                    "p (h t) -> p h t", h=HEADS)[
                                    :, w * 8:(w + 1) * 8, :],
                                pAT[:].rearrange("p (h t) -> p h t", h=8),
                                _bc(mAT_s[:, None, :], [128, 8, C]),
                                ALU.mult), sig=sV, deps=[mm])
                            guard["pAT"] = atm_ev
                            mm = None
                            for i in range(8):
                                h = w * 8 + i
                                dep0 = ([atm_ev, guard.get("S_bf")]
                                        + vgtok[c]) if i == 0 else []
                                qT.do(lambda t, i=i, h=h: t.matmul(
                                    pO[:, h * DV:(h + 1) * DV],
                                    qgT_hf[0:32, i * C:(i + 1) * C],
                                    S_bf[0:32, h * DV:(h + 1) * DV],
                                    start=True, stop=False), deps=dep0)
                                mm = qT.do(lambda t, h=h, c=c: t.matmul(
                                    pO[:, h * DV:(h + 1) * DV],
                                    atm_c[:, h * C:(h + 1) * C],
                                    vTok[c % 2][:, h * DV:(h + 1) * DV],
                                    start=False, stop=True),
                                    sig=sT if i == 7 else None)
                            o_first.append(mm)
                        o_done = o_first[-1]
                        if c + 1 < NC1:
                            vg_proj(c + 1)
                            prep = gla_prep(c + 1)
                        bw(qS, "pAT")
                        a3 = qS.do(lambda s: s.activation(
                            pAT[:, 0:1024], pO[:, 0:1024], AF.Square),
                            sig=sS, deps=[o_done])
                        v4 = qV.do(lambda v: v.reduce_sum(
                            ss16[:], pAT[:].rearrange(
                                "p (h v) -> p h v", h=HEADS),
                            axis=AX.X), sig=sV,
                            deps=[a3, guard.get("ss_a")])
                        guard["pAT"] = v4
                        v4 = qV.do(lambda v: v.tensor_scalar(
                            ss16[:], ss16[:], 1.0 / DV, EPS,
                            ALU.mult, ALU.add), sig=sV, deps=[v4])
                        a4 = qS.do(lambda s: s.activation(
                            rg16[:], ss16[:], AF.Sqrt), sig=sS, deps=[v4])
                        guard["ss_a"] = a4
                        v4 = qV.do(lambda v: v.reciprocal(
                            ri16[:], rg16[:]), sig=sV, deps=[a4])
                        og = atm_c[:, 0:D]
                        v5 = qV.do(lambda v, c=c: v.tensor_tensor(
                            og, pO[:, 0:D], gs[c % 2][:],
                            ALU.mult), sig=sV, deps=[v4] + vgtok[c])
                        guard[f"gs{c % 2}"] = v5
                        v5 = qV.do(lambda v: v.tensor_tensor(
                            og.rearrange("p (h v) -> p h v", h=HEADS),
                            og.rearrange("p (h v) -> p h v", h=HEADS),
                            _bc(ri16[:, :, None], [128, HEADS, DV]),
                            ALU.mult), sig=sV, deps=[v5])
                        og_done = v5
                        tap("og", og, 128, glob_c, [og_done], D)
                        bw(qT, "pO")
                        mm = None
                        for h in range(HEADS):
                            mm = qT.do(lambda t, h=h, c=c, par=par: t.matmul(
                                pO[0:32, h * DV:(h + 1) * DV],
                                kdt_c[par][:, 32 * h:32 * (h + 1)],
                                vTok[c % 2][:, h * DV:(h + 1) * DV],
                                start=True, stop=True),
                                sig=sT if h == HEADS - 1 else None,
                                deps=[og_done] if h == 0 else ())
                        sup = mm
                        p1 = qP.do(lambda g, par=par: g.tensor_tensor(
                            S_sb[:].rearrange("p (h v) -> p h v", h=HEADS),
                            S_sb[:].rearrange("p (h v) -> p h v", h=HEADS),
                            _bc(dec_sb[par][:, :, None], [32, HEADS, DV]),
                            ALU.mult), sig=sP, deps=[dv1, og_done])
                        guard[f"dec_p{par}"] = p1
                        v6 = qV.do(lambda v: v.tensor_add(
                            S_sb[:], S_sb[:], pO[0:32, :]), sig=sV,
                            deps=[sup, p1])
                        v6 = qV.do(lambda v: v.tensor_copy(
                            S_bf[:], S_sb[:]), sig=sV, deps=[v6])
                        guard["pO"] = v6
                        guard["S_bf"] = v6
                        ogt_ev = transpose_to(
                            lambda: ogT_c[:],
                            lambda b: og[:, b * C:(b + 1) * C], 8,
                            [og_done])
                        for hf in range(2):
                            pb = pP[hf]
                            bw(qT, f"pP{hf}")
                            mm = None
                            for fc in range(8):
                                mm = qT.do(
                                    lambda t, fc=fc, hf=hf, pb=pb:
                                    t.matmul(
                                        pb[:, 0:512],
                                        ogT_c[:, fc * C:(fc + 1) * C],
                                        wo_s[:].rearrange(
                                            "p (k n) -> p k n", k=8)[
                                            :, fc,
                                            hf * 512:hf * 512 + 512],
                                        start=(fc == 0), stop=(fc == 7)),
                                    sig=sT if fc == 7 else None,
                                    deps=[ogt_ev] if fc == 0 else ())
                            ev = qV.do(lambda v, c=c, hf=hf, pb=pb:
                                       v.scalar_tensor_tensor(
                                           xs[:, c * D + hf * 512:
                                              c * D + hf * 512 + 512],
                                           pb[:, 0:512], 1.0,
                                           xs[:, c * D + hf * 512:
                                              c * D + hf * 512 + 512],
                                           ALU.mult, ALU.add), sig=sV,
                                       deps=[mm])
                            guard[f"pP{hf}"] = ev
                    xb_done = sV.tok
                    tap("xB", xs[:, 0:D], 128, st, [xb_done], D)
                if P1L >= 6:
                    ln_block(lambda c: xs[:, c * D:(c + 1) * D], False, NC1,
                             hT, "h2", st * NC1, src_tok=xb_done)
                    h2_all = [sV.tok]
                    ln2_stats_done = sS.tok
                    addt = {}
                    for c in (2, 3):
                        addt[c] = qV.do(lambda v, c=c: v.tensor_add(
                            xs[:, c * D:(c + 1) * D], xs[:, c * D:(c + 1) * D],
                            xc_buf[c % 2][:]), sig=sV, deps=[ln2_stats_done])
                    for c in (0, 1):
                        rlt = qS.do(lambda s, c=c, n0=n0: s.dma_start(
                            out=xc_buf[c % 2][:],
                            in_=xin_h.rearrange("(n p) f -> p n f", p=128)[
                                :, n0 + c, :]), sig=(dX if c % 2 == 0 else dA),
                            deps=[addt[c + 2]])
                        addt[c] = qV.do(lambda v, c=c: v.tensor_add(
                            xs[:, c * D:(c + 1) * D], xs[:, c * D:(c + 1) * D],
                            xc_buf[c % 2][:]), sig=sV, deps=[rlt])
                    guard["xc0"] = guard["xcv0"] = addt[0]
                    guard["xc1"] = guard["xcv1"] = addt[1]
                    qV.wait(addt[1])
                    if (phases & 2) and st >= 1 and st - 1 < E:
                        e = st - 1
                        pf_tok[e] = [qP.do(
                            lambda g, d=dc, s=sf, e=e, f=f_: g.dma_start(
                                out=d[e].rearrange(
                                    "p (m n) -> p m n", m=f),
                                in_=s[e].rearrange(
                                    "(m p) n -> p m n", p=128)),
                            sig=eSs[e % 3][j])
                            for j, (sf, dc, f_) in enumerate(
                                ((ewg_h, ewgc_h, 8), (ewu_h, ewuc_h, 8),
                                 (ewd_h, ewdc_h, 4)))]

                if P1L >= 7:
                    w2ev = {}
                    for qu in range(4):
                        wb = w1b[qu % 2]
                        gq = gelu_q[qu % 2]
                        lt = qY.do(lambda y, qu=qu, wb=wb: y.dma_start(
                            out=wb[:], in_=w1c_h[qu]),
                            sig=w1y[qu % 2],
                            deps=[guard.get(f"w1b{qu % 2}"), w1pf[qu]])
                        for uc in range(4):
                            pb = pP[uc % 2]
                            bw(qT, f"pP{uc % 2}")
                            mm = None
                            for k in range(8):
                                mm = qT.do(lambda t, k=k, uc=uc, wb=wb, pb=pb:
                                           t.matmul(
                                               pb[:, 0:ST1],
                                               wb[:].rearrange(
                                                   "p (k u q) -> p k u q",
                                                   k=8, u=4)[:, k, uc, :],
                                               hT[:, k * ST1:(k + 1) * ST1],
                                               start=(k == 0), stop=(k == 7)),
                                           sig=sT if k == 7 else None,
                                           deps=(h2_all + [lt]) if k == 0 else ())
                            ev = qS.do(lambda s, u=qu * 4 + uc, uc=uc, pb=pb,
                                       gq=gq: s.activation(
                                gq[:, uc * ST1:(uc + 1) * ST1], pb[:, 0:ST1],
                                AF.Gelu, bias=b1col[:, u:u + 1]), sig=sS,
                                deps=[mm, guard.get(f"gelu{qu % 2}")])
                            guard[f"pP{uc % 2}"] = ev
                        guard[f"w1b{qu % 2}"] = sT.tok
                        gelu_done = sS.tok
                        if qu == 0 and "gelu" in taps:
                            tap("gelu", gq[:, 0:ST1], 128, st, [gelu_done], ST1)
                        wb2 = w2b[qu % 2]
                        lt2 = qY.do(lambda y, qu=qu, wb2=wb2: y.dma_start(
                            out=wb2[:], in_=w2c_h[qu]),
                            sig=w2y[qu % 2],
                            deps=[guard.get(f"w2b{qu % 2}"), w2pf[qu]])
                        for c in range(NC1):
                            for hf in range(2):
                                pb = pP[(c * 2 + hf) % 2]
                                bw(qT, f"pP{(c * 2 + hf) % 2}")
                                mm = None
                                for uc in range(4):
                                    last = (uc == 3 and qu != 0)
                                    mm = qT.do(lambda t, uc=uc, c=c, hf=hf,
                                               wb2=wb2, pb=pb, last=last, gq=gq:
                                               t.matmul(
                                                   pb[:, 0:512],
                                                   gq[:].rearrange(
                                                       "p (u t) -> p u t",
                                                       u=4)[:, uc,
                                                            c * C:(c + 1) * C],
                                                   wb2[:].rearrange(
                                                       "p (u h f) -> p u h f",
                                                       u=4, h=2)[:, uc, hf, :],
                                                   start=(uc == 0), stop=last),
                                               sig=sT if last else None,
                                               deps=[gelu_done, lt2]
                                               if uc == 0 else ())
                                if qu == 0:
                                    mm = qT.do(lambda t, hf=hf, pb=pb: t.matmul(
                                        pb[:, 0:512], onesb_s[0:1, :],
                                        b2row[0:1, hf * 512:hf * 512 + 512],
                                        start=False, stop=True), sig=sT)
                                ev = qV.do(lambda v, c=c, hf=hf, pb=pb:
                                           v.scalar_tensor_tensor(
                                               xs[:, c * D + hf * 512:
                                                  c * D + hf * 512 + 512],
                                               pb[:, 0:512], 1.0,
                                               xs[:, c * D + hf * 512:
                                                  c * D + hf * 512 + 512],
                                               ALU.mult, ALU.add), sig=sV,
                                           deps=[mm])
                                guard[f"pP{(c * 2 + hf) % 2}"] = ev
                                if qu == 3 and hf == 1:
                                    w2ev[c] = ev
                        guard[f"w2b{qu % 2}"] = sT.tok
                        guard[f"gelu{qu % 2}"] = sT.tok
                        if qu == 1 and P1L >= 8 and st + 1 < NST1:
                            n1 = (st + 1) * NC1
                            pd = {}
                            for c in (0, 1):
                                lt = qS.do(lambda s, c=c, n1=n1: s.dma_start(
                                    out=xc_buf[c][:],
                                    in_=xin_h.rearrange(
                                        "(n p) f -> p n f", p=128)[
                                        :, n1 + c, :]),
                                    sig=(dX2 if c == 0 else dA2),
                                    deps=[guard.get(f"xc{c}"),
                                          guard.get(f"xcv{c}")])
                                qV.wait(lt)
                                qS.wait(lt)
                                guard[f"lnst{c}"] = sV.tok
                                guard[f"lnsa{c}"] = sS.tok
                                pd[c] = ln_stats(xc_buf[c][:], True, c)
                            pre_ln1[st + 1] = pd

                if P1L >= 8:
                    for c in range(NC1):
                        stok = qS.do(lambda s, c=c, n0=n0: s.dma_start(
                            out=x1scr.rearrange("(n p) f -> p n f", p=128)[
                                :, n0 + c, :],
                            in_=xs[:, c * D:(c + 1) * D]), sig=dS,
                            deps=[w2ev[c]])
                        x1_stored.append(stok)
                        guard[f"xs_store{c}"] = stok
                    tap("x1", xs[:, 0:D], 128, st, [guard["xs_store0"]], D)

            phase1_end = [sT.tok, sV.tok, sS.tok, sP.tok, dS.tok, dY.tok, dP.tok,
                          dR.tok, dX.tok, dA.tok, dX2.tok, dA2.tok, dCW.tok,
                          w1s[0].tok, w1s[1].tok, w2s[0].tok, w2s[1].tok]
            flush()

        nsub_ = ST2_ // 512
        es2 = ExitStack()
        with es2:
            x1t_ = es2.enter_context(
                nc.sbuf_tensor("x1t", [128, NC2 * D], F32))
            x1t = [x1t_, x1t_]
            x1c = es2.enter_context(nc.sbuf_tensor("x1c", [128, D], F32))
            h3T = [es2.enter_context(
                nc.sbuf_tensor(f"h3T{i}", [128, 8 * ST2_], BF16))
                for i in range(2)]
            route = [es2.enter_context(
                nc.sbuf_tensor(f"route{i}", [128, NC2 * E], F32))
                for i in range(2)]
            scores = es2.enter_context(nc.sbuf_tensor("scores", [128, E], F32))
            masked = es2.enter_context(nc.sbuf_tensor("masked", [128, E], F32))
            mx8 = es2.enter_context(nc.sbuf_tensor("mx8", [128, 8], F32))
            mrscr = es2.enter_context(nc.sbuf_tensor("mrscr", [128, 8], F32))
            w12 = es2.enter_context(nc.sbuf_tensor("w12", [128, 2], F32))
            eqv = es2.enter_context(nc.sbuf_tensor("eqv", [128, E], F32))
            indv = es2.enter_context(nc.sbuf_tensor("indv", [128, E], F32))
            p_sb = [es2.enter_context(
                nc.sbuf_tensor(f"p_sb{i}", [128, nsub_ * 4 * 512], BF16))
                for i in range(2)]
            sil_sb = [es2.enter_context(
                nc.sbuf_tensor(f"sil_sb{i}", [128, 2 * 512], BF16))
                for i in range(2)]
            gwt_s = es2.enter_context(nc.sbuf_tensor("gwt_s", [128, 8 * E], F32))
            h3f = es2.enter_context(nc.sbuf_tensor("h3f", [128, D], F32))
            h3Tf = es2.enter_context(nc.sbuf_tensor("h3Tf", [128, D], F32))
            swg_s = es2.enter_context(nc.sbuf_tensor("swg_s", [128, 8 * 512], BF16))
            swu_s = es2.enter_context(nc.sbuf_tensor("swu_s", [128, 8 * 512], BF16))
            swd_s = es2.enter_context(nc.sbuf_tensor("swd_s", [128, 4 * 1024], BF16))
            ewgb = [es2.enter_context(
                nc.sbuf_tensor(f"ewgb{i}", [128, 8 * 512], BF16))
                for i in range(3)]
            ewub = [es2.enter_context(
                nc.sbuf_tensor(f"ewub{i}", [128, 8 * 512], BF16))
                for i in range(3)]
            ewdb = [es2.enter_context(
                nc.sbuf_tensor(f"ewdb{i}", [128, 4 * 1024], BF16))
                for i in range(3)]
            s2_neg = es2.enter_context(nc.sbuf_tensor("s2_neg", [128, 1], F32))
            s2_ssq = es2.enter_context(nc.sbuf_tensor("s2_ssq", [128, 1], F32))
            s2_a = es2.enter_context(nc.sbuf_tensor("s2_a", [128, 1], F32))
            s2_b = es2.enter_context(nc.sbuf_tensor("s2_b", [128, 1], F32))
            s2_c2 = es2.enter_context(nc.sbuf_tensor("s2_c2", [128, 1], F32))

            qP.do(lambda g: g.dma_start(
                out=gwt_s[:].rearrange("p (k e) -> p k e", k=8),
                in_=gwt_h.rearrange("(k p) e -> p k e", p=128)),
                sig=dP, deps=phase1_end)
            qP.do(lambda g: g.dma_start(
                out=swg_s[:].rearrange("p (k n) -> p k n", k=8),
                in_=swg_h.rearrange("(k p) n -> p k n", p=128)),
                sig=dP)
            qP.do(lambda g: g.dma_start(
                out=swu_s[:].rearrange("p (k n) -> p k n", k=8),
                in_=swu_h.rearrange("(k p) n -> p k n", p=128)),
                sig=dP)
            qP.do(lambda g: g.dma_start(
                out=swd_s[:].rearrange("p (m n) -> p m n", m=4),
                in_=swd_h.rearrange("(m p) n -> p m n", p=128)),
                sig=dP)
            p2w_ready = dP.tok

            cw_tok = {}

            def exp_load(e, bi, first_pass, deps=()):
                toks = list(deps)
                out_toks = []
                for j, (bufs, src_f, src_c, kind) in enumerate((
                    (ewgb, ewg_h, ewgc_h, "g"), (ewub, ewu_h, ewuc_h, "g"),
                    (ewdb, ewd_h, ewdc_h, "d"),
                )):
                    dst = bufs[bi]
                    if kind == "g":
                        dv = dst[:].rearrange("p (k n) -> p k n", k=8)
                        sf = src_f[e].rearrange("(k p) n -> p k n", p=128)
                    else:
                        dv = dst[:].rearrange("p (m n) -> p m n", m=4)
                        sf = src_f[e].rearrange("(m p) n -> p m n", p=128)
                    if e in pf_tok and not EXP_NOCACHE:
                        t = qY.do(lambda y, d=dst, s=src_c, e=e: y.dma_start(
                            out=d[:], in_=s[e]), sig=eYs[bi][j],
                            deps=toks + [pf_tok[e][j]])
                    else:
                        t = qP.do(lambda g, d=dv, s=sf: g.dma_start(
                            out=d, in_=s), sig=eSs[bi][j], deps=toks)
                    out_toks.append(t)
                return out_toks

            eload = {(0, 0): (exp_load(0, 0, True, deps=phase1_end), 0),
                     (0, 1): (exp_load(1, 1, True, deps=phase1_end), 1)}

            pre_tok = {}

            def prelude_chunk(st, c):
                b = st % 2
                n0 = st * (ST2_ // 128)
                if True:
                    xc = x1c[:]
                    xlc = qS.do(lambda s, n0=n0, c=c: s.dma_start(
                        out=x1c[:],
                        in_=x1scr.rearrange("(n p) f -> p n f", p=128)[
                            :, n0 + c, :]), sig=dX,
                        deps=[guard.get("x1c")] + phase1_end)
                    qV.wait(xlc)
                    qS.wait(xlc)
                    r = qV.do(lambda v, xc=xc: v.reduce_sum(
                        s2_neg[:], xc, axis=AX.X, negate=True), sig=sV,
                        deps=[guard.get("ln2g")])
                    r = qV.do(lambda v: v.tensor_scalar_mul(
                        s2_neg[:], s2_neg[:], 1.0 / D), sig=sV, deps=[r])
                    bw(qS, "pQx")
                    a = qS.do(lambda s, xc=xc: s.activation(
                        pQx[:, 0:512], xc[:, 0:512], AF.Square,
                        bias=s2_neg[:, 0:1], accum_out=s2_ssq[:]), sig=sS,
                        deps=[r])
                    a = qS.do(lambda s, xc=xc: s.activation(
                        pQx[:, 0:512], xc[:, 512:1024], AF.Square,
                        bias=s2_neg[:, 0:1], accum_out=s2_a[:]), sig=sS,
                        deps=[a])
                    guard["pQx"] = a
                    r = qV.do(lambda v: v.tensor_add(
                        s2_ssq[:], s2_ssq[:], s2_a[:]), sig=sV, deps=[a])
                    r = qV.do(lambda v: v.tensor_scalar(
                        s2_b[:], s2_ssq[:], 1.0 / D, EPS, ALU.mult, ALU.add),
                        sig=sV, deps=[r])
                    a = qS.do(lambda s: s.activation(
                        s2_c2[:], s2_b[:], AF.Sqrt), sig=sS, deps=[r])
                    guard["ln2g"] = a
                    r = qV.do(lambda v: v.reciprocal(s2_b[:], s2_c2[:]),
                              sig=sV, deps=[a])
                    h1 = qV.do(lambda v, xc=xc: v.tensor_scalar(
                        h3f[:], xc, s2_neg[:, 0:1], s2_b[:, 0:1],
                        ALU.add, ALU.mult), sig=sV,
                        deps=[r, guard.get("h3f")])
                    guard["x1c"] = h1
                    tap("h3", h3f[:], 128, st * NC2 + c, [h1], D)
                    for rd in range(2):
                        bw(qT, "pQx")
                        t0 = None
                        for i in range(4):
                            t0 = qT.do(lambda t, i=i, rd=rd: t.transpose(
                                pQx[:, i * C:(i + 1) * C],
                                h3f[:, (rd * 4 + i) * C:
                                    (rd * 4 + i + 1) * C], idenf_s[:]),
                                sig=sT if i == 3 else None,
                                deps=[h1, guard.get("h3Tf"),
                                      guard.get("h3f")] if i == 0 else ())
                        evf = qV.do(lambda v, rd=rd: v.tensor_copy(
                            h3Tf[:, rd * 512:(rd + 1) * 512],
                            pQx[:, 0:512]), sig=sV, deps=[t0])
                        guard["pQx"] = evf
                        guard["h3f"] = sT.tok
                    hv = qV.do(lambda v, c=c, b=b: v.tensor_copy(
                        h3T[b][:].rearrange("p (k t) -> p k t", k=8)[
                            :, :, c * C:(c + 1) * C],
                        h3Tf[:].rearrange("p (k t) -> p k t", k=8)),
                        sig=sV, deps=[evf])
                    bw(qT, "pQx")
                    mm = None
                    for k in range(8):
                        mm = qT.do(lambda t, k=k: t.matmul(
                            pQx[:, 0:E],
                            h3Tf[:, k * C:(k + 1) * C],
                            gwt_s[:].rearrange("p (k e) -> p k e", k=8)[
                                :, k, :],
                            start=(k == 0), stop=(k == 7)),
                            sig=sT if k == 7 else None,
                            deps=[evf, p2w_ready] if k == 0 else ())
                    guard["h3Tf"] = [mm, hv]
                    a = qS.do(lambda s: s.activation(
                        scores[:], pQx[:, 0:E], AF.Sigmoid), sig=sS,
                        deps=[mm, guard.get("topk")])
                    guard["pQx"] = a
                    v1 = qV.do(lambda v: v.max(out=mx8[:], in_=scores[:]),
                               sig=sV, deps=[a])
                    v1 = qV.do(lambda v: v.tensor_copy(
                        mrscr[:, 0:2], mx8[:, 0:2]), sig=sV, deps=[v1])
                    v1 = qV.do(lambda v: v.memset(mrscr[:, 2:8], -1.0),
                               sig=sV, deps=[v1])
                    v1 = qV.do(lambda v: v.match_replace(
                        out=masked[:], in_to_replace=mrscr[:],
                        in_values=scores[:], imm_value=-2.0), sig=sV,
                        deps=[v1])
                    v1 = qV.do(lambda v: v.tensor_tensor(
                        indv[:], scores[:], masked[:], ALU.not_equal),
                        sig=sV, deps=[v1])
                    v1 = qV.do(lambda v: v.tensor_scalar(
                        eqv[:], scores[:], mx8[:, 0:1], None, ALU.is_equal),
                        sig=sV, deps=[v1])
                    v1 = qV.do(lambda v: v.tensor_add(
                        w12[:, 0:1], mx8[:, 0:1], mx8[:, 1:2]), sig=sV,
                        deps=[v1])
                    v1 = qV.do(lambda v: v.tensor_scalar_add(
                        w12[:, 0:1], w12[:, 0:1], 1e-20), sig=sV, deps=[v1])
                    v1 = qV.do(lambda v: v.reciprocal(
                        w12[:, 1:2], w12[:, 0:1]), sig=sV, deps=[v1])
                    v1 = qV.do(lambda v: v.tensor_scalar(
                        w12[:, 0:2], mx8[:, 0:2], w12[:, 1:2], 2.5,
                        ALU.mult, ALU.mult), sig=sV, deps=[v1])
                    v1 = qV.do(lambda v: v.tensor_sub(
                        indv[:], indv[:], eqv[:]), sig=sV, deps=[v1])
                    v1 = qV.do(lambda v: v.tensor_scalar_mul(
                        indv[:], indv[:], w12[:, 1:2]), sig=sV, deps=[v1])
                    v1 = qV.do(lambda v: v.tensor_scalar_mul(
                        eqv[:], eqv[:], w12[:, 0:1]), sig=sV, deps=[v1])
                    v1 = qV.do(lambda v, c=c, b=b: v.tensor_add(
                        route[b][:, c * E:(c + 1) * E], eqv[:], indv[:]),
                        sig=sV, deps=[v1])
                    guard["topk"] = v1
                    tap("route", route[b][:, c * E:(c + 1) * E], 128,
                        st * NC2 + c, [v1], E)
                pre_tok[(st, c)] = [sV.tok]

            def prelude(st):
                for c in range(NC2):
                    prelude_chunk(st, c)

            def gup(ei, st, ldtoks):
                b = st % 2
                par = ei % 2
                wgb = swg_s if ei == E else ewgb[eb_of[(st, ei)]]
                wub = swu_s if ei == E else ewub[eb_of[(st, ei)]]
                for sb in range(nsub):
                    tcol = sb * 512
                    for mc in range(4):
                        bw(qT, "pP0")
                        mm = None
                        for k in range(8):
                            mm = qT.do(
                                lambda t, k=k, mc=mc, tcol=tcol, wgb=wgb,
                                b=b: t.matmul(
                                    pP0[:, 0:512],
                                    wgb[:].rearrange(
                                        "p (k m q) -> p k m q",
                                        k=8, m=4)[:, k, mc, :],
                                    h3T[b][:, k * ST2_ + tcol:
                                           k * ST2_ + tcol + 512],
                                    start=(k == 0), stop=(k == 7)),
                                sig=sT if k == 7 else None,
                                deps=(pre_tok[(st, NC2 - 1)] + ldtoks)
                                if k == 0 else ())
                        sl = mc % 2
                        sev = qS.do(lambda s, par=par, sl=sl: s.activation(
                            sil_sb[par][:, sl * 512:sl * 512 + 512],
                            pP0[:, 0:512], AF.Silu), sig=sS,
                            deps=[mm, guard.get(f"sil{par}{sl}")])
                        guard["pP0"] = sev
                        bw(qT, "pP1")
                        mm = None
                        for k in range(8):
                            mm = qT.do(
                                lambda t, k=k, mc=mc, tcol=tcol, wub=wub,
                                b=b: t.matmul(
                                    pP1[:, 0:512],
                                    wub[:].rearrange(
                                        "p (k m q) -> p k m q",
                                        k=8, m=4)[:, k, mc, :],
                                    h3T[b][:, k * ST2_ + tcol:
                                           k * ST2_ + tcol + 512],
                                    start=(k == 0), stop=(k == 7)),
                                sig=sT if k == 7 else None)
                        pev = qV.do(lambda v, mc=mc, sb=sb, par=par, sl=sl:
                                    v.tensor_tensor(
                                        p_sb[par][:, (sb * 4 + mc) * 512:
                                                  (sb * 4 + mc + 1) * 512],
                                        pP1[:, 0:512],
                                        sil_sb[par][:, sl * 512:sl * 512 + 512],
                                        ALU.mult), sig=sV, deps=[mm, sev])
                        guard["pP1"] = pev
                        guard[f"sil{par}{sl}"] = pev
                return sV.tok

            def down(ei, st, p_done):
                b = st % 2
                par = ei % 2
                wdb = swd_s if ei == E else ewdb[eb_of[(st, ei)]]
                for sb in range(nsub):
                    for tc in range(4):
                        ch = sb * 4 + tc
                        for hf in range(2):
                            pb = pAT if hf == 0 else pO
                            nm = "pAT" if hf == 0 else "pO"
                            bw(qT, nm)
                            mm = None
                            for mc in range(4):
                                mm = qT.do(
                                    lambda t, mc=mc, ch=ch, hf=hf, pb=pb,
                                    wdb=wdb, par=par: t.matmul(
                                        pb[:, 0:512],
                                        p_sb[par][:].rearrange(
                                            "p (m t) -> p m t",
                                            m=nsub * 4)[
                                            :, (ch // 4) * 4 + mc,
                                            (ch % 4) * C:(ch % 4 + 1) * C],
                                        wdb[:].rearrange(
                                            "p (m h f) -> p m h f",
                                            m=4, h=2)[:, mc, hf, :],
                                        start=(mc == 0), stop=(mc == 3)),
                                    sig=sT if mc == 3 else None,
                                    deps=[p_done] if mc == 0 else ())
                            qV.wait(x1t_ld[st])
                            if ei == E:
                                ev = qV.do(
                                    lambda v, ch=ch, hf=hf, pb=pb, b=b:
                                    v.scalar_tensor_tensor(
                                        x1t[b][:, ch * D + hf * 512:
                                               ch * D + hf * 512 + 512],
                                        pb[:, 0:512], 1.0,
                                        x1t[b][:, ch * D + hf * 512:
                                               ch * D + hf * 512 + 512],
                                        ALU.mult, ALU.add), sig=sV,
                                    deps=[mm])
                            else:
                                ev = qV.do(
                                    lambda v, ch=ch, hf=hf, pb=pb, ei=ei,
                                    b=b: v.scalar_tensor_tensor(
                                        x1t[b][:, ch * D + hf * 512:
                                               ch * D + hf * 512 + 512],
                                        pb[:, 0:512],
                                        route[b][:, ch * E + ei:
                                                 ch * E + ei + 1],
                                        x1t[b][:, ch * D + hf * 512:
                                               ch * D + hf * 512 + 512],
                                        ALU.mult, ALU.add), sig=sV,
                                    deps=[mm])
                            guard[nm] = ev
                if ei != E:
                    guard[f"eb{eb_of[(st, ei)]}"] = sT.tok

            nsub = ST2_ // 512
            eb_of = {}
            x1t_ld = {}
            prelude(0)
            for st in range(NST2):
                n0_ = st * (ST2_ // 128)
                x1t_ld[st] = qS.do(lambda s, n0=n0_: s.dma_start(
                    out=x1t_[:].rearrange("p (n f) -> p n f", n=NC2),
                    in_=x1scr.rearrange("(n p) f -> p n f", p=128)[
                        :, n0:n0 + NC2, :]), sig=dS,
                    deps=[guard.get("x1t")] + phase1_end)
                pend = None
                for ei in range(E + 1):
                    if ei != E:
                        ldtoks, bi = eload.pop((st, ei))
                        eb_of[(st, ei)] = bi
                    else:
                        ldtoks = [p2w_ready]
                    pt = gup(ei, st, ldtoks)
                    if pend is not None:
                        down(pend[0], st, pend[1])
                    if ei != E:
                        gi = st * E + ei
                        nx = gi + 2
                        nxt = (nx // E, nx % E)
                        if nxt[0] < NST2 and nxt not in eload:
                            eload[nxt] = (exp_load(
                                nxt[1], nx % 3,
                                first_pass=(nxt[0] == 0 or EXP_NOCACHE),
                                deps=[guard.get(f"eb{nx % 3}")]),
                                nx % 3)
                    if 2 <= ei < 2 + NC2 and st + 1 < NST2:
                        prelude_chunk(st + 1, ei - 2)
                    pend = (ei, pt)
                down(pend[0], st, pend[1])
                out_done = sV.tok
                b = st % 2
                n0 = st * (ST2_ // 128)
                stok = qS.do(lambda s, n0=n0, b=b: s.dma_start(
                    out=out_h.rearrange("(n p) f -> p n f", p=128)[
                        :, n0:n0 + NC2, :],
                    in_=x1t_[:].rearrange("p (n f) -> p n f", n=NC2)),
                    sig=dS, deps=[out_done])
                guard["x1t"] = stok

            flush()

        fin_sems = [dS, dY, dP, dR, dX, dA, dX2, dA2, dCW, dEC, dT,
                    w1s[0], w1s[1], w2s[0], w2s[1],
                    w1y[0], w1y[1], w2y[0], w2y[1],
                    sT, sV, sS] + [c for sl in eSs + eYs for c in sl]
        blk.gpsimd(lambda g: [g.wait_ge(c.h, c.n) for c in fin_sems if c.n])
    return nc


def _stage_inputs(inputs):
    f = np.float32
    d = {}
    d["Wq"] = np.ascontiguousarray(inputs["L0_attn_Wq"], f)
    d["Wk"] = np.ascontiguousarray(inputs["L0_attn_Wk"], f)
    d["Wv"] = np.ascontiguousarray(inputs["L0_attn_Wv"], f)
    d["Wg"] = np.ascontiguousarray(inputs["L0_attn_Wg"], f)
    d["Wo"] = np.ascontiguousarray(inputs["L0_attn_Wo"], f)
    d["W1"] = np.ascontiguousarray(inputs["L0_attn_W1"], f)
    d["W2"] = np.ascontiguousarray(inputs["L0_attn_W2"], f)
    gk1 = np.asarray(inputs["L0_attn_gk_w1"], f)
    d["gk1pad"] = np.concatenate([gk1, np.zeros((D, 16), f)], axis=1)
    gk2 = np.asarray(inputs["L0_attn_gk_w2"], f)
    gkb = np.asarray(inputs["L0_attn_gk_b2"], f).reshape(1, QF)
    d["gk2ext"] = np.concatenate([gk2, gkb, np.zeros((15, QF), f)], axis=0)
    d["ln_w"] = np.asarray(inputs["L0_attn_ln_w"], f).reshape(1, D)
    d["ln_b"] = np.asarray(inputs["L0_attn_ln_b"], f).reshape(1, D)
    d["rms_w"] = np.asarray(inputs["L0_attn_rms_w"], f).reshape(1, DV)
    d["b1"] = np.ascontiguousarray(
        np.asarray(inputs["L0_attn_b1"], f).reshape(16, 128).T)
    d["b2"] = np.asarray(inputs["L0_attn_b2"], f).reshape(1, D)
    d["gate_wT"] = np.ascontiguousarray(np.asarray(inputs["L0_ffn_gate_w"], f).T)
    d["e_wg"] = np.ascontiguousarray(inputs["L0_ffn_w_gate"], f)
    d["e_wu"] = np.ascontiguousarray(inputs["L0_ffn_w_up"], f)
    d["e_wd"] = np.ascontiguousarray(inputs["L0_ffn_w_down"], f)
    d["ws_gate"] = np.ascontiguousarray(inputs["L0_ffn_ws_gate"], f)
    d["ws_up"] = np.ascontiguousarray(inputs["L0_ffn_ws_up"], f)
    d["ws_down"] = np.ascontiguousarray(inputs["L0_ffn_ws_down"], f)
    return d


def kernel(**inputs):
    x = np.asarray(inputs["L0_attn_x"], np.float32)
    B, T, _ = x.shape
    shared = _stage_inputs(inputs)
    nc = build_program(T, taps=DEBUG_TAPS)
    in_maps = []
    for c in range(B):
        m = {"x": np.ascontiguousarray(x[c])}
        m.update(shared)
        in_maps.append(m)
    res = run_bass_kernel_spmd(nc, in_maps, core_ids=list(range(B)))
    out = np.stack([r["out"] for r in res.results], axis=0)
    return out.astype(np.float32)
